# version 11
# baseline (speedup 1.0000x reference)
import os

import numpy as np

import concourse.bass as bass
import concourse.mybir as mybir
from concourse.bass_utils import run_bass_kernel_spmd

F32 = mybir.dt.float32
BF16 = mybir.dt.bfloat16
I32 = mybir.dt.int32
AF = mybir.ActivationFunctionType
ALU = mybir.AluOpType
AX = mybir.AxisListType

L = 2048
DM = 1024
E = 2048
N = 16
R = 64
HID = 4096
TT = 512
NTILE = L // TT
DCH = E // 128
KD = DM // 128
NTC = L // 128
W96 = R + 2 * N
INV_LN2 = float(1.0 / np.log(2.0))
LN2 = float(np.log(2.0))
TS = TT + 1
MAGIC = float(1.5 * 2.0 ** 23)
LN_EPS = 1e-5

ENGS = ("sp", "pe", "act", "dve", "pool")


class Plan:
    def __init__(self):
        self.q = {e: [] for e in ENGS}
        self.cnt = {e: 0 for e in ENGS}
        self.dma_cnt = {e: 0 for e in ENGS}
        self.pending = {e: [] for e in ENGS}

    def op(self, eng, fn, waits=()):
        ws = [x for x in waits if x is not None] + self.pending[eng]
        self.pending[eng] = []
        rec = {"fn": fn, "waits": ws, "inc": None, "dma": False}
        self.q[eng].append(rec)
        return rec

    def ev(self, eng):
        rec = self.q[eng][-1]
        assert not rec["dma"]
        if rec["inc"] is None:
            self.cnt[eng] += 1
            rec["inc"] = self.cnt[eng]
        return ("e", eng, rec["inc"])

    def dma(self, eng, fn, waits=()):
        ws = [x for x in waits if x is not None] + self.pending[eng]
        self.pending[eng] = []
        rec = {"fn": fn, "waits": ws, "inc": None, "dma": True}
        self.q[eng].append(rec)
        self.dma_cnt[eng] += 16
        rec["inc"] = self.dma_cnt[eng]
        return ("d", eng, rec["inc"])

    def last_event(self, eng):
        if not self.q[eng]:
            return None
        rec = self.q[eng][-1]
        if rec["dma"]:
            return ("d", eng, rec["inc"])
        return self.ev(eng)

    def barrier(self):
        evs = [self.last_event(e) for e in ENGS]
        evs = [x for x in evs if x is not None]
        for e in ENGS:
            own = [x for x in evs if x[1] != e]
            self.pending[e].extend(own)


def build_program(stop_after="full"):
    nc = bass.Bass("TRN2", target_bir_lowering=False, detect_race_conditions=False)
    P = Plan()

    x_in = nc.dram_tensor("x", [L, DM], F32, kind="ExternalInput").ap()
    w = {}
    for d in ("f", "b"):
        w[d + "inw"] = nc.dram_tensor(d + "inw", [2 * E, DM], F32, kind="ExternalInput").ap()
        w[d + "convw"] = nc.dram_tensor(d + "convw", [E, 4], F32, kind="ExternalInput").ap()
        w[d + "convb"] = nc.dram_tensor(d + "convb", [E], F32, kind="ExternalInput").ap()
        w[d + "xproj"] = nc.dram_tensor(d + "xproj", [W96, E], F32, kind="ExternalInput").ap()
        w[d + "dtw"] = nc.dram_tensor(d + "dtw", [E, R], F32, kind="ExternalInput").ap()
        w[d + "dtb"] = nc.dram_tensor(d + "dtb", [E], F32, kind="ExternalInput").ap()
        w[d + "alog"] = nc.dram_tensor(d + "alog", [E, N], F32, kind="ExternalInput").ap()
        w[d + "dpar"] = nc.dram_tensor(d + "dpar", [E], F32, kind="ExternalInput").ap()
        w[d + "outw"] = nc.dram_tensor(d + "outw", [DM, E], F32, kind="ExternalInput").ap()
    w["wfc"] = nc.dram_tensor("wfc", [HID, DM], F32, kind="ExternalInput").ap()
    w["bfc"] = nc.dram_tensor("bfc", [HID], F32, kind="ExternalInput").ap()
    w["wproj"] = nc.dram_tensor("wproj", [DM, HID], F32, kind="ExternalInput").ap()
    w["bproj"] = nc.dram_tensor("bproj", [DM], F32, kind="ExternalInput").ap()
    out = nc.dram_tensor("out", [L, DM], F32, kind="ExternalOutput").ap()

    xfm_d = nc.dram_tensor("xfm_d", [128, KD, L], BF16).ap()
    inT_d = {d: nc.dram_tensor(d + "inT_d", [128, KD, 2 * E], BF16).ap() for d in ("f", "b")}
    outT_d = {d: nc.dram_tensor(d + "outT_d", [128, DCH, DM], BF16).ap() for d in ("f", "b")}
    wfcT_d = nc.dram_tensor("wfcT_d", [128, KD, HID], BF16).ap()
    wprojT_d = nc.dram_tensor("wprojT_d", [128, HID // 128, DM], BF16).ap()
    y_d = {d: nc.dram_tensor(d + "y_d", [128, NTC, DM], F32).ap() for d in ("f", "b")}

    scopes = {}

    def sb(name, shape, dtype, scope="persist"):
        cm = nc.sbuf_tensor(name, shape, dtype)
        t = cm.__enter__()
        scopes.setdefault(scope, []).append(cm)
        return t

    def ps(name, shape, dtype):
        cm = nc.psum_tensor(name, shape, dtype)
        t = cm.__enter__()
        scopes.setdefault("persist", []).append(cm)
        return t

    def close_scope(scope):
        for cm in reversed(scopes.pop(scope, [])):
            cm.__exit__(None, None, None)

    ident = sb("ident", [128, 128], BF16)
    ident32 = sb("ident32", [128, 128], F32)
    sel16 = sb("sel16", [16, N * 128], BF16)
    ones1 = sb("ones1", [1, 128], BF16)
    A_sb = sb("A_sb", [128, 2 * DCH * N], F32)
    D_sb = sb("D_sb", [128, 2 * DCH], F32)
    convw_sb = sb("convw_sb", [128, 2 * DCH * 4], F32)
    convb_sb = sb("convb_sb", [128, 2 * DCH], F32)
    dtb_sb = sb("dtb_sb", [128, 2 * DCH], F32)
    xprojT_sb = sb("xprojT_sb", [128, 2 * DCH * W96], BF16)
    dtwT_sb = sb("dtwT_sb", [64, 2 * E], BF16)
    hcarry = sb("hcarry", [128, DCH * N], F32)
    eps_sb = sb("eps_sb", [128, 1], F32)
    qc_sb = sb("qc_sb", [128, 8], F32)
    QC_VALS = (-0.5 + 1e-5, MAGIC, -MAGIC, 100.0, -103.0, INV_LN2)
    for ci, val in enumerate(QC_VALS):
        nc.const_aps.register(F32, float(val), qc_sb[:, ci:ci + 1])
    STR = sb("STR", [128, 8192], BF16)

    pz = [ps(f"pz{i}", [128, 512], F32) for i in range(4)]
    ptr = [ps(f"ptr{i}", [128, 512], BF16) for i in range(2)]
    ptr32 = [ps(f"ptr32_{i}", [128, 512], F32) for i in range(2)]
    pz_war = [None] * 4
    ptr_war = [None] * 2
    ptr32_war = [None] * 2
    pz_i = [0]
    ptr_i = [0]
    ptr32_i = [0]

    def rot(bufs, wars, idx):
        i = idx[0] % len(bufs)
        idx[0] += 1
        waits = [wars[i]] if wars[i] is not None else []
        return i, bufs[i], waits

    last = {}

    def lw(key):
        return last.get(key)

    bigf_pair = [sb(f"big_f32{i}", [128, 2048], F32, "prep") for i in range(3)]
    bigbf_pair = [sb(f"big_bf{i}", [128, 4096], BF16, "prep") for i in range(3)]
    big_f32 = bigf_pair[0]
    big2_f32 = sb("big2_f32", [128, 2048], F32, "prep")
    big_bf = bigbf_pair[0]
    prep_i = [0]

    def prep_bufs():
        p = prep_i[0] % 3
        prep_i[0] += 1
        return p, bigf_pair[p], bigbf_pair[p]
    qs_amax = sb("qs_amax", [128, 128], F32, "prep")
    qs_e = sb("qs_e", [128, 128], F32, "prep")
    qs_scale = sb("qs_scale", [128, 128], F32, "prep")
    qs_inv = sb("qs_inv", [128, 128], F32, "prep")
    qs_two = sb("qs_two", [128, 128], F32, "prep")
    qs_r = sb("qs_r", [128, 2048], F32, "prep")
    qs_i = sb("qs_i", [128, 2048], I32, "prep")

    def quant_rows(v_ap, ncol, e_ready, out_bf_ap, Q):
        nb = ncol // 32
        P.op("dve", lambda v: v.tensor_reduce(
            out=Q["two"][:, 0:nb], in_=v_ap.rearrange("p (g k) -> p g k", k=32),
            axis=AX.X, op=ALU.max, apply_absolute_value=True),
            waits=list(e_ready))
        e_red = P.ev("dve")
        P.op("dve", lambda v: v.tensor_scalar_max(
            out=Q["amax"][:, 0:nb], in0=Q["two"][:, 0:nb], scalar1=float(2.0 ** -126)),
            waits=[e_red])
        e_am = P.ev("dve")
        P.op("act", lambda a: a.activation(out=Q["e"][:, 0:nb], in_=Q["amax"][:, 0:nb],
                                           func=AF.Ln), waits=[e_am])
        P.op("act", lambda a: a.activation(out=Q["scale"][:, 0:nb], in_=Q["e"][:, 0:nb],
                                           func=AF.Copy, scale=INV_LN2, bias=-0.5 + 1e-5))
        P.op("act", lambda a: a.activation(out=Q["inv"][:, 0:nb], in_=Q["scale"][:, 0:nb],
                                           func=AF.Copy, bias=MAGIC))
        P.op("act", lambda a: a.activation(out=Q["e"][:, 0:nb], in_=Q["inv"][:, 0:nb],
                                           func=AF.Copy, bias=-MAGIC))
        P.op("act", lambda a: a.activation(out=Q["scale"][:, 0:nb], in_=Q["e"][:, 0:nb],
                                           func=AF.Relu, bias=100.0))
        P.op("act", lambda a: a.activation(out=Q["inv"][:, 0:nb], in_=Q["scale"][:, 0:nb],
                                           func=AF.Copy, bias=-103.0))
        P.op("act", lambda a: a.activation(out=Q["amax"][:, 0:nb], in_=Q["inv"][:, 0:nb],
                                           func=AF.Exp, scale=LN2))
        P.op("act", lambda a: a.activation(out=Q["e"][:, 0:nb], in_=Q["inv"][:, 0:nb],
                                           func=AF.Exp, scale=-LN2))
        e_exp = P.ev("act")
        inv_b = Q["e"][:, 0:nb][:, :, None].broadcast_to([128, nb, 32])
        sc_b = Q["amax"][:, 0:nb][:, :, None].broadcast_to([128, nb, 32])
        v3 = v_ap.rearrange("p (g k) -> p g k", k=32)
        r3 = Q["r"][:, 0:ncol].rearrange("p (g k) -> p g k", k=32)
        P.op("dve", lambda v: v.tensor_tensor(out=r3, in0=v3, in1=inv_b, op=ALU.mult),
             waits=[e_exp])
        P.op("dve", lambda v: v.tensor_scalar(
            out=Q["r"][:, 0:ncol], in0=Q["r"][:, 0:ncol], scalar1=MAGIC, scalar2=-MAGIC,
            op0=ALU.add, op1=ALU.add))
        P.op("dve", lambda v: v.tensor_copy(out=Q["i"][:, 0:32], in_=Q["r"][:, 0:32]))
        P.op("dve", lambda v: v.tensor_tensor(
            out=out_bf_ap.rearrange("p (g k) -> p g k", k=32), in0=r3, in1=sc_b, op=ALU.mult))
        return P.ev("dve")

    QP = {"amax": qs_amax, "e": qs_e, "scale": qs_scale, "inv": qs_inv,
          "two": qs_two, "r": qs_r, "i": qs_i}

    P.op("pool", lambda g: g.iota(big_f32[:, 0:128], [[1, 128]], channel_multiplier=-1,
                                  allow_small_or_imprecise_dtypes=True))
    e_i0 = P.ev("pool")
    P.op("dve", lambda v: v.tensor_single_scalar(out=big2_f32[:, 0:128], in_=big_f32[:, 0:128],
                                                 scalar=0.0, op=ALU.is_equal), waits=[e_i0])
    P.op("dve", lambda v: v.tensor_copy(out=ident[:], in_=big2_f32[:, 0:128]))
    P.op("dve", lambda v: v.tensor_copy(out=ident32[:], in_=big2_f32[:, 0:128]))
    e_ident = P.ev("dve")
    P.op("pool", lambda g: g.iota(big_f32[0:16, 0:N * 128], [[1, N], [0, 128]],
                                  channel_multiplier=0, allow_small_or_imprecise_dtypes=True),
         waits=[e_ident])
    P.op("pool", lambda g: g.iota(big2_f32[0:16, 0:N * 128], [[0, N * 128]],
                                  channel_multiplier=1, allow_small_or_imprecise_dtypes=True))
    e_i1 = P.ev("pool")
    P.op("dve", lambda v: v.tensor_tensor(out=big_f32[0:16, 0:N * 128],
                                          in0=big_f32[0:16, 0:N * 128],
                                          in1=big2_f32[0:16, 0:N * 128], op=ALU.is_equal),
         waits=[e_i1])
    P.op("dve", lambda v: v.tensor_copy(out=sel16[:], in_=big_f32[0:16, 0:N * 128]))
    e_sel = P.ev("dve")
    P.op("dve", lambda v: v.memset(ones1[:], 1.0))
    e_ones = P.ev("dve")
    P.op("dve", lambda v: v.memset(eps_sb[:], LN_EPS))
    P.op("dve", lambda v: v.memset(qc_sb[:, 0:1], -0.5 + 1e-5))
    P.op("dve", lambda v: v.memset(qc_sb[:, 1:2], MAGIC))
    P.op("dve", lambda v: v.memset(qc_sb[:, 2:3], -MAGIC))
    P.op("dve", lambda v: v.memset(qc_sb[:, 3:4], 100.0))
    P.op("dve", lambda v: v.memset(qc_sb[:, 4:5], -103.0))
    P.op("dve", lambda v: v.memset(qc_sb[:, 5:6], INV_LN2))
    e_eps = P.ev("dve")

    for di, d in enumerate(("f", "b")):
        o = di * DCH
        P.dma("sp", lambda s, di=di, d=d: s.dma_start(
            out=A_sb[:, di * DCH * N:(di + 1) * DCH * N].rearrange("p (c n) -> p c n", n=N),
            in_=w[d + "alog"].rearrange("(c p) n -> p c n", p=128)))
        if not os.environ.get("NOSLOWDMA"):
            P.dma("sp", lambda s, o=o, d=d: s.dma_start(
                out=D_sb[:, o:o + DCH], in_=w[d + "dpar"].rearrange("(c p) -> p c", p=128),
                allow_slow_non_contiguous=True))
        P.dma("sp", lambda s, di=di, d=d: s.dma_start(
            out=convw_sb[:, di * DCH * 4:(di + 1) * DCH * 4].rearrange("p (c k) -> p c k", k=4),
            in_=w[d + "convw"].rearrange("(c p) k -> p c k", p=128)))
        if not os.environ.get("NOSLOWDMA"):
            P.dma("sp", lambda s, o=o, d=d: s.dma_start(
                out=convb_sb[:, o:o + DCH], in_=w[d + "convb"].rearrange("(c p) -> p c", p=128),
                allow_slow_non_contiguous=True))
        if not os.environ.get("NOSLOWDMA"):
            e_par = P.dma("sp", lambda s, o=o, d=d: s.dma_start(
                out=dtb_sb[:, o:o + DCH], in_=w[d + "dtb"].rearrange("(c p) -> p c", p=128),
                allow_slow_non_contiguous=True))
        else:
            e_par = P.dma("sp", lambda s, o=o, d=d: s.dma_start(
                out=big_f32[0:1, 0:E], in_=w[d + "dtb"][None, :]))
    P.op("act", lambda a: a.activation(out=A_sb[:], in_=A_sb[:], func=AF.Exp), waits=[e_par])
    P.op("dve", lambda v: v.tensor_scalar_mul(out=A_sb[:], in0=A_sb[:], scalar1=-1.0),
         waits=[P.ev("act")])
    e_A = P.ev("dve")

    P.barrier()

    def chunk_transpose_spill(src_row_ap, ncol, dst_ap_fn, quant=False):
        p, bf32, bbf = prep_bufs()
        e_l = P.dma("sp", lambda s, src=src_row_ap, ncol=ncol, bf32=bf32: s.dma_start(
            out=bf32[:, 0:ncol], in_=src), waits=[lw(f"pf{p}")])
        if quant:
            e_q = quant_rows(bf32[:, 0:ncol], ncol,
                             [e_l, lw(f"ptr_rd{p}")], bbf[:, 0:ncol], QP)
        else:
            P.op("dve", lambda v, ncol=ncol, bf32=bf32, bbf=bbf: v.tensor_copy(
                out=bbf[:, 0:ncol], in_=bf32[:, 0:ncol]),
                waits=[e_l, lw(f"ptr_rd{p}")])
            e_q = P.ev("dve")
        last[f"pf{p}"] = e_q
        kb = ncol // 128
        for g in range(kb // 4):
            j, pt, wwaits = rot(ptr, ptr_war, ptr_i)
            for q in range(4):
                k = g * 4 + q
                P.op("pe", lambda t, k=k, q=q, pt=pt, bbf=bbf: t.transpose(
                    out=pt[:, q * 128:(q + 1) * 128], in_=bbf[:, k * 128:(k + 1) * 128],
                    identity=ident[:]),
                    waits=([e_q, e_ident] + wwaits) if q == 0 else [])
            e_tr = P.ev("pe")
            P.op("dve", lambda v, g=g, pt=pt, bbf=bbf: v.tensor_copy(
                out=bbf[:, 2048 + g * 512:2048 + (g + 1) * 512], in_=pt[:]),
                waits=[e_tr] + ([lw(f"pspill{p}")] if g == 0 else []))
            ptr_war[j] = P.ev("dve")
        last[f"ptr_rd{p}"] = P.ev("pe")
        e_s = P.ev("dve")
        last[f"pspill{p}"] = P.dma(
            "pool", lambda s, kb=kb, ncol=ncol, bbf=bbf: s.dma_start(
                out=dst_ap_fn(),
                in_=bbf[:, 2048:2048 + ncol].rearrange("p (k q) -> p k q", k=kb)),
            waits=[e_s])
        return last[f"pspill{p}"]

    for tc in range(NTC):
        chunk_transpose_spill(
            x_in[tc * 128:(tc + 1) * 128, :], DM,
            lambda tc=tc: xfm_d[:, :, tc * 128:(tc + 1) * 128])

    if stop_after == "prepx1":
        stop_after = "prepx"
    else:
      for d in ("f", "b"):
        for fc in range(2 * E // 128):
            chunk_transpose_spill(w[d + "inw"][fc * 128:(fc + 1) * 128, :], DM,
                                  lambda d=d, fc=fc: inT_d[d][:, :, fc * 128:(fc + 1) * 128])
        for fc in range(KD):
            chunk_transpose_spill(w[d + "outw"][fc * 128:(fc + 1) * 128, :], E,
                                  lambda d=d, fc=fc: outT_d[d][:, :, fc * 128:(fc + 1) * 128])
        di = 0 if d == "f" else 1
        p, bf32, bbf = prep_bufs()
        e_l = P.dma("sp", lambda s, d=d, bf32=bf32: s.dma_start(
            out=bf32[0:96, 0:E], in_=w[d + "xproj"]), waits=[lw(f"pf{p}")])
        P.op("dve", lambda v, bf32=bf32, bbf=bbf: v.tensor_copy(
            out=bbf[0:96, 0:E], in_=bf32[0:96, 0:E]),
            waits=[e_l, lw(f"ptr_rd{p}")])
        e_c = P.ev("dve")
        last[f"pf{p}"] = e_c
        for g in range(DCH // 4):
            j, pt, wwaits = rot(ptr, ptr_war, ptr_i)
            for q in range(4):
                c = g * 4 + q
                P.op("pe", lambda t, c=c, q=q, pt=pt, bbf=bbf: t.transpose(
                    out=pt[0:128, q * 96:q * 96 + 96], in_=bbf[0:96, c * 128:(c + 1) * 128],
                    identity=ident[0:96, 0:96]),
                    waits=([e_c, e_ident] + wwaits) if q == 0 else [])
            e_tr = P.ev("pe")
            P.op("dve", lambda v, g=g, pt=pt, di=di: v.tensor_copy(
                out=xprojT_sb[:, (di * DCH + g * 4) * W96:(di * DCH + g * 4 + 4) * W96],
                in_=pt[0:128, 0:4 * 96]), waits=[e_tr])
            ptr_war[j] = P.ev("dve")
        last[f"ptr_rd{p}"] = P.ev("pe")
        p, bf32, bbf = prep_bufs()
        e_l = P.dma("sp", lambda s, d=d, bf32=bf32: s.dma_start(
            out=bf32[:, 0:DCH * R].rearrange("p (c r) -> p c r", r=R),
            in_=w[d + "dtw"].rearrange("(c p) r -> p c r", p=128)),
            waits=[lw(f"pf{p}")])
        P.op("dve", lambda v, bf32=bf32, bbf=bbf: v.tensor_copy(
            out=bbf[:, 0:DCH * R], in_=bf32[:, 0:DCH * R]),
            waits=[e_l, lw(f"ptr_rd{p}")])
        e_c = P.ev("dve")
        last[f"pf{p}"] = e_c
        for g in range(DCH // 4):
            j, pt, wwaits = rot(ptr, ptr_war, ptr_i)
            for q in range(4):
                c = g * 4 + q
                P.op("pe", lambda t, c=c, q=q, pt=pt, bbf=bbf: t.transpose(
                    out=pt[0:64, q * 128:(q + 1) * 128], in_=bbf[:, c * R:(c + 1) * R],
                    identity=ident[:]),
                    waits=([e_c, e_ident] + wwaits) if q == 0 else [])
            e_tr = P.ev("pe")
            P.op("dve", lambda v, g=g, pt=pt, di=di: v.tensor_copy(
                out=dtwT_sb[0:64, di * E + g * 512:di * E + (g + 1) * 512],
                in_=pt[0:64, 0:512]), waits=[e_tr])
            ptr_war[j] = P.ev("dve")
        last[f"ptr_rd{p}"] = P.ev("pe")

    if stop_after != "prepx":
        for fc in range(HID // 128):
            chunk_transpose_spill(w["wfc"][fc * 128:(fc + 1) * 128, :], DM,
                                  lambda fc=fc: wfcT_d[:, :, fc * 128:(fc + 1) * 128], quant=True)
        for fc in range(DM // 128):
            for half in range(2):
                chunk_transpose_spill(
                    w["wproj"][fc * 128:(fc + 1) * 128, half * 2048:(half + 1) * 2048], 2048,
                    lambda fc=fc, half=half: wprojT_d[:, half * DCH:(half + 1) * DCH,
                                                      fc * 128:(fc + 1) * 128], quant=True)
    e_prep = None

    if stop_after in ("prepw", "prepx"):
        srcten = inT_d["f"] if stop_after == "prepw" else xfm_d
        P.barrier()
        stg_bf = sb("dstg_bf", [128, DM], BF16, "dump")
        stg_f = sb("dstg_f", [128, DM], F32, "dump")
        e_p = None
        e_c2 = None
        for j in range(16):
            Kn = KD
            kk = j % Kn
            c0 = (j // Kn) * DM
            e_l = P.dma("sp", lambda s, kk=kk, c0=c0, srcten=srcten: s.dma_start(
                out=stg_bf[:], in_=srcten[:, kk, c0:c0 + DM]), waits=[e_p])
            P.op("dve", lambda v: v.tensor_copy(out=stg_f[:], in_=stg_bf[:]),
                 waits=[e_l] + ([e_c2] if e_c2 else []))
            e_c2 = P.ev("dve")
            e_p = P.dma("sp", lambda s, j=j: s.dma_start(
                out=out[j * 128:(j + 1) * 128, :], in_=stg_f[:]), waits=[e_c2])
        P.op("pool", lambda g: g.nop(), waits=[e_p])
        return nc, P, None

    close_scope("prep")
    P.barrier()

    xh_sb = sb("xh_sb", [128, DCH * TT], BF16, "mamba")
    halo_sb = sb("halo_sb", [128, DCH * 4], BF16, "mamba")
    xc_sb = sb("xc_sb", [128, DCH * TT], BF16, "mamba")
    z_sb = sb("z_sb", [128, DCH * TT], BF16, "mamba")
    bball_sb = sb("bball_sb", [128, N * TT], BF16, "mamba")
    cball_sb = sb("cball_sb", [128, N * TT], BF16, "mamba")
    dt_sb = sb("dt_sb", [64, TT], BF16, "mamba")
    delta_sb = [sb(f"delta{i}", [128, TT], F32, "mamba") for i in range(2)]
    spe_sb = sb("spe", [128, TT], F32, "mamba")
    dAall_sb = sb("dAall", [128, N * TS], BF16, "mamba")
    cacc_sb = sb("cacc", [128, 2 * TT], F32, "mamba")
    dxc_sb = sb("dxc", [128, TT], BF16, "mamba")
    dbuall_sb = sb("dbuall", [128, N * TS], BF16, "mamba")
    bc_rows = sb("bc_rows", [16, 2 * TT], BF16, "mamba")
    ysc_sb = [sb(f"ysc{i}", [128, TT], BF16, "mamba") for i in range(2)]
    y2_sb = sb("y2", [128, TT], BF16, "mamba")
    sz_sb = sb("sz", [128, TT], BF16, "mamba")
    stage_sb = sb("stage_sb", [128, TT], F32, "mamba")
    piece_sb = [sb("piece0", [128, 512], F32, "mamba")]
    xfmt_sb = sb("xfmt_sb", [128, KD * TT], BF16, "mamba")


    def dump_fm(bufs, waits):
        stg = cacc_sb
        row = 0
        e_p = None
        ws = list(waits)
        for buf, ncol in bufs:
            for j in range(ncol // DM):
                P.op("dve", lambda v, buf=buf, j=j: v.tensor_copy(
                    out=stg[:], in_=buf[:, j * DM:(j + 1) * DM]),
                    waits=ws + ([e_p] if e_p else []))
                ws = []
                e_c = P.ev("dve")
                e_p = P.dma("sp", lambda s, row=row: s.dma_start(
                    out=out[row * 128:(row + 1) * 128, :], in_=stg[:]), waits=[e_c])
                row += 1
        P.op("pool", lambda g: g.nop(), waits=[e_p])
        return "HALT"

    pending_out = [None]

    def mamba_dir(d):
        di = 0 if d == "f" else 1
        rev = d == "b"
        tiles = list(range(NTILE - 1, -1, -1)) if rev else list(range(NTILE))

        P.op("dve", lambda v: v.memset(hcarry[:], 0.0))
        P.op("dve", lambda v: v.memset(halo_sb[:], 0.0))
        P.op("dve", lambda v: v.memset(
            dAall_sb[:].rearrange("p (n t) -> p n t", n=N)[:, :, 0], 0.0))
        e_init = P.ev("dve")
        e_yg = e_init

        for ti, tile in enumerate(tiles):
            t0 = tile * TT
            e_xf = P.dma("sp", lambda g, t0=t0: g.dma_start(
                out=xfmt_sb[:].rearrange("p (k q) -> p k q", k=KD),
                in_=xfm_d[:, :, t0:t0 + TT]),
                waits=[e_prep, lw("xfmt")])

            for fg in range(8):
                if fg == 4 and pending_out[0] is not None:
                    t0p, e_gp = pending_out[0]
                    pending_out[0] = None
                    emit_out_proj(d, t0p, e_gp)
                wb = STR[:, (fg % 2) * 4096:(fg % 2) * 4096 + 4096]
                e_w = P.dma("sp", lambda s, d=d, fg=fg, wb=wb: s.dma_start(
                    out=wb.rearrange("p (k q) -> p k q", k=KD),
                    in_=inT_d[d][:, :, fg * 512:(fg + 1) * 512]),
                    waits=[e_prep, lw(f"str{fg % 2}")])
                psl = [rot(pz, pz_war, pz_i) for _ in range(4)]
                for k in range(KD):
                    for fi in range(4):
                        i, pb, wwaits = psl[fi]
                        P.op("pe", lambda t, wb=wb, pb=pb, fi=fi, k=k: t.matmul(
                            out=pb[:], lhsT=wb[:, k * 512 + fi * 128:k * 512 + (fi + 1) * 128],
                            rhs=xfmt_sb[:, k * TT:(k + 1) * TT],
                            start=(k == 0), stop=(k == KD - 1)),
                            waits=([e_w, e_xf] + wwaits) if k == 0 else [])
                e_mm = P.ev("pe")
                last[f"str{fg % 2}"] = e_mm
                last["xfmt"] = e_mm
                for fi in range(4):
                    i, pb, _ = psl[fi]
                    fo = fg * 4 + fi
                    if fo < DCH:
                        c = fo
                        first = [e_mm, lw("xh_used"), lw("tree_war")] if fi == 0 else []
                        if rev:
                            P.op("dve", lambda v, c=c, pb=pb: v.tensor_copy(
                                out=xh_sb[:, c * TT:(c + 1) * TT][:, ::-1], in_=pb[:]),
                                waits=first)
                        else:
                            P.op("act", lambda a, c=c, pb=pb: a.activation(
                                out=xh_sb[:, c * TT:(c + 1) * TT], in_=pb[:], func=AF.Copy),
                                waits=first)
                    else:
                        c = fo - DCH
                        P.op("act", lambda a, c=c, pb=pb: a.activation(
                            out=z_sb[:, c * TT:(c + 1) * TT], in_=pb[:], func=AF.Copy),
                            waits=[e_mm, lw("z_used")] if fi == 0 else [])
                e_ev = P.ev("dve" if (rev and fg < 4) else "act")
                for fi in range(4):
                    pz_war[psl[fi][0]] = e_ev
            e_xh_a = P.ev("act")
            e_xh_d = P.ev("dve") if rev else None
            e_z = e_xh_a
            if stop_after == "xh0" and ti == 0:
                return dump_fm([(xh_sb, DCH * TT), (z_sb, DCH * TT)], [e_xh_a, e_xh_d])

            for c in range(DCH):
                cacc = cacc_sb[:, (c % 2) * TT:(c % 2 + 1) * TT]
                wcol = lambda k, c=c: convw_sb[:, (di * DCH + c) * 4 + k:(di * DCH + c) * 4 + k + 1]
                halo_c = halo_sb[:, c * 4:c * 4 + 3]
                xh_c = xh_sb[:, c * TT:(c + 1) * TT]
                P.op("dve", lambda v, xh_c=xh_c, cacc=cacc, wcol=wcol: v.tensor_scalar_mul(
                    out=cacc[:], in0=xh_c, scalar1=wcol(3)),
                    waits=[e_xh_a, e_xh_d, lw(f"cacc{c % 2}")])
                for k in range(3):
                    nh = 3 - k
                    P.op("dve", lambda v, k=k, nh=nh, halo_c=halo_c, cacc=cacc, wcol=wcol:
                         v.scalar_tensor_tensor(
                             out=cacc[:, 0:nh], in0=halo_c[:, k:3], scalar=wcol(k),
                             in1=cacc[:, 0:nh], op0=ALU.mult, op1=ALU.add))
                    P.op("dve", lambda v, k=k, nh=nh, xh_c=xh_c, cacc=cacc, wcol=wcol:
                         v.scalar_tensor_tensor(
                             out=cacc[:, nh:TT], in0=xh_c[:, 0:TT - nh], scalar=wcol(k),
                             in1=cacc[:, nh:TT], op0=ALU.mult, op1=ALU.add))
                e_cv = P.ev("dve")
                P.op("dve", lambda v, c=c, xh_c=xh_c: v.tensor_copy(
                    out=halo_sb[:, c * 4:c * 4 + 3], in_=xh_c[:, TT - 3:TT]))
                P.op("act", lambda a, c=c, cacc=cacc, di=di: a.activation(
                    out=xc_sb[:, c * TT:(c + 1) * TT], in_=cacc[:], func=AF.Silu,
                    bias=convb_sb[:, di * DCH + c:di * DCH + c + 1]),
                    waits=[e_cv, lw(f"xcg{c}")])
                last[f"cacc{c % 2}"] = P.ev("act")
            e_xc = P.ev("act")
            if stop_after == "xc0" and ti == 0:
                return dump_fm([(xc_sb, DCH * TT)], [e_xc])

            groups = [(0, R, 64), (R, R + N, 16), (R + N, W96, 16)]
            e_gmm = []
            pgrp = []
            for gi, (c0, c1, rows) in enumerate(groups):
                i, pb, wwaits = rot(pz, pz_war, pz_i)
                pgrp.append((i, pb))
                for c in range(DCH):
                    P.op("pe", lambda t, c=c, c0=c0, c1=c1, rows=rows, pb=pb, di=di: t.matmul(
                        out=pb[0:rows, :],
                        lhsT=xprojT_sb[:, (di * DCH + c) * W96 + c0:(di * DCH + c) * W96 + c1],
                        rhs=xc_sb[:, c * TT:(c + 1) * TT],
                        start=(c == 0), stop=(c == DCH - 1)),
                        waits=([e_xc] + wwaits) if c == 0 else [])
                e_gmm.append(P.ev("pe"))
            i_dt, pb_dt = pgrp[0]
            P.op("act", lambda a, pb=pb_dt: a.activation(out=dt_sb[0:64, :], in_=pb[0:64, :],
                                                         func=AF.Copy),
                 waits=[e_gmm[0], lw("dt_used")])
            e_dt = P.ev("act")
            pz_war[i_dt] = e_dt
            iB, pbB = pgrp[1]
            iC, pbC = pgrp[2]
            P.op("act", lambda a, pb=pbB: a.activation(out=bc_rows[0:16, 0:TT], in_=pb[0:16, :],
                                                       func=AF.Copy),
                 waits=[e_gmm[1], lw("bcsrc_used")])
            pz_war[iB] = P.ev("act")
            P.op("act", lambda a, pb=pbC: a.activation(out=bc_rows[0:16, TT:2 * TT],
                                                       in_=pb[0:16, :], func=AF.Copy),
                 waits=[e_gmm[2]])
            pz_war[iC] = P.ev("act")
            e_bcsrc = P.ev("act")

            bc_jobs = [(dst, off, n) for n in range(N)
                       for (dst, off) in ((bball_sb, 0), (cball_sb, TT))]
            for g0 in range(0, len(bc_jobs), 4):
                grp = []
                for gi, (dst, off, n) in enumerate(bc_jobs[g0:g0 + 4]):
                    i, pb, wwaits = rot(pz, pz_war, pz_i)
                    grp.append((i, pb, dst, n))
                    P.op("pe", lambda t, n=n, pb=pb, off=off: t.matmul(
                        out=pb[:], lhsT=sel16[:, n * 128:(n + 1) * 128],
                        rhs=bc_rows[0:16, off:off + TT], start=True, stop=True),
                        waits=[e_bcsrc if g0 == 0 and gi == 0 else None, e_sel] + wwaits)
                e_bm = P.ev("pe")
                for ci, (i, pb, dst, n) in enumerate(grp):
                    P.op("dve", lambda v, n=n, pb=pb, dst=dst: v.tensor_copy(
                        out=dst[:, n * TT:(n + 1) * TT], in_=pb[:]),
                        waits=[e_bm if ci == 0 else None])
                e_g = P.ev("dve")
                for (i, pb, dst, n) in grp:
                    pz_war[i] = e_g
            e_bcall = P.ev("dve")
            last["bcsrc_used"] = P.ev("pe")
            if stop_after == "bc0" and ti == 0:
                return dump_fm([(bball_sb, N * TT), (cball_sb, N * TT), (dt_sb, 0)],
                               [e_bcall, e_dt])

            dA3 = dAall_sb[:].rearrange("p (n t) -> p n t", n=N)
            dbu3 = dbuall_sb[:].rearrange("p (n t) -> p n t", n=N)
            hx3 = xh_sb[:].rearrange("p (n t) -> p n t", n=N)

            def emit_gate(c, e_tree):
                P.op("dve", lambda v, c=c, di=di: v.scalar_tensor_tensor(
                    out=y2_sb[:], in0=xc_sb[:, c * TT:(c + 1) * TT],
                    scalar=D_sb[:, di * DCH + c:di * DCH + c + 1], in1=ysc_sb[c % 2][:],
                    op0=ALU.mult, op1=ALU.add),
                    waits=[e_tree])
                P.op("dve", lambda v, c=c, rev=rev: v.tensor_tensor(
                    out=z_sb[:, c * TT:(c + 1) * TT],
                    in0=(y2_sb[:, ::-1] if rev else y2_sb[:]), in1=sz_sb[:], op=ALU.mult),
                    waits=[lw("sz_ev")])
                e_yg = P.ev("dve")
                last["sz_used"] = e_yg
                last["ysc_war"] = e_yg
                last[f"xcg{c}"] = e_yg
                return e_yg

            for c in range(DCH):
                dbuf = delta_sb[c % 2]
                i, pb, wwaits = rot(pz, pz_war, pz_i)
                P.op("pe", lambda t, c=c, pb=pb, di=di: t.matmul(
                    out=pb[:], lhsT=dtwT_sb[0:64, di * E + c * 128:di * E + (c + 1) * 128],
                    rhs=dt_sb[0:64, :], start=True, stop=True),
                    waits=[e_dt, lw(f"delta_used{c % 2}")] + wwaits)
                e_dmm = P.ev("pe")
                if c == DCH - 1:
                    last["dt_used"] = e_dmm
                P.op("act", lambda a, c=c, pb=pb, di=di: a.activation(
                    out=spe_sb[:], in_=pb[:], func=AF.Exp,
                    bias=dtb_sb[:, di * DCH + c:di * DCH + c + 1]),
                    waits=[e_dmm])
                pz_war[i] = P.ev("act")
                P.op("act", lambda a, dbuf=dbuf: a.activation(
                    out=dbuf[:], in_=spe_sb[:], func=AF.Ln, bias=1.0))
                e_delta = P.ev("act")
                if c > 0:
                    P.op("act", lambda a, c=c: a.activation(
                        out=sz_sb[:], in_=z_sb[:, (c - 1) * TT:c * TT], func=AF.Silu),
                        waits=[e_z, lw("sz_used")])
                    last["sz_ev"] = P.ev("act")
                P.op("dve", lambda v, c=c, dbuf=dbuf: v.tensor_tensor(
                    out=dxc_sb[:], in0=dbuf[:], in1=xc_sb[:, c * TT:(c + 1) * TT], op=ALU.mult),
                    waits=[e_delta])
                P.op("dve", lambda v, c=c: v.tensor_copy(
                    out=dbu3[:, :, 0], in_=hcarry[:, c * N:(c + 1) * N]))
                P.op("dve", lambda v: v.tensor_tensor(
                    out=dbu3[:, :, 1:TS],
                    in0=dxc_sb[:][:, None, :].broadcast_to([128, N, TT]),
                    in1=bball_sb[:].rearrange("p (n t) -> p n t", n=N), op=ALU.mult),
                    waits=[e_bcall if c == 0 else None])
                e_dAh = []
                for h in range(2):
                    for j in range(8):
                        n = h * 8 + j
                        P.op("act", lambda a, c=c, n=n, dbuf=dbuf, di=di: a.activation(
                            out=dA3[:, n, 1:TS], in_=dbuf[:], func=AF.Exp,
                            scale=A_sb[:, (di * DCH + c) * N + n:(di * DCH + c) * N + n + 1]),
                            waits=[lw(f"dA_war{h}") if j == 0 else None])
                    e_dAh.append(P.ev("act"))
                for h in range(2):
                    o0 = h * 8 * TS
                    P.op("dve", lambda v, o0=o0: v.tensor_tensor_scan(
                        out=dbuall_sb[:, o0:o0 + 8 * TS], data0=dAall_sb[:, o0:o0 + 8 * TS],
                        data1=dbuall_sb[:, o0:o0 + 8 * TS], initial=0.0,
                        op0=ALU.mult, op1=ALU.add),
                        waits=[e_dAh[h]])
                    last[f"dA_war{h}"] = P.ev("dve")
                e_scan = lw("dA_war1")
                last[f"delta_used{c % 2}"] = e_scan
                P.op("dve", lambda v, c=c: v.tensor_copy(
                    out=hcarry[:, c * N:(c + 1) * N], in_=dbu3[:, :, TS - 1]))
                P.op("dve", lambda v, hx3=hx3: v.tensor_tensor(
                    out=hx3[:], in0=cball_sb[:].rearrange("p (n t) -> p n t", n=N),
                    in1=dbu3[:, :, 1:TS], op=ALU.mult))
                e_hm = P.ev("dve")
                P.op("pool", lambda g, hx3=hx3: g.tensor_tensor(
                    out=hx3[:, 0:8], in0=hx3[:, 0:8],
                    in1=hx3[:, 8:16], op=ALU.add),
                    waits=[e_hm])
                P.op("pool", lambda g, hx3=hx3: g.tensor_tensor(
                    out=hx3[:, 0:4], in0=hx3[:, 0:4],
                    in1=hx3[:, 4:8], op=ALU.add))
                P.op("pool", lambda g, hx3=hx3: g.tensor_tensor(
                    out=hx3[:, 0:2], in0=hx3[:, 0:2],
                    in1=hx3[:, 2:4], op=ALU.add))
                P.op("pool", lambda g, c=c, hx3=hx3: g.tensor_tensor(
                    out=ysc_sb[c % 2][:], in0=hx3[:, 0], in1=hx3[:, 1],
                    op=ALU.add),
                    waits=[lw("ysc_war")])
                e_tree_c = P.ev("pool")
                last["tree_war"] = e_tree_c
                if c > 0:
                    emit_gate(c - 1, e_tree_prev)
                e_tree_prev = e_tree_c
            P.op("act", lambda a: a.activation(
                out=sz_sb[:], in_=z_sb[:, (DCH - 1) * TT:DCH * TT], func=AF.Silu),
                waits=[e_z, lw("sz_used")])
            last["sz_ev"] = P.ev("act")
            e_gate = emit_gate(DCH - 1, e_tree_prev)
            if stop_after == "y0" and ti == 0:
                return dump_fm([(z_sb, DCH * TT)], [e_gate])
            last["xh_used"] = e_gate
            pending_out[0] = (t0, e_gate)
        t0p, e_gp = pending_out[0]
        pending_out[0] = None
        emit_out_proj(d, t0p, e_gp)
        return None

    def emit_out_proj(d, t0, e_gate):
        for fg in range(2):
            e_w = P.dma("sp", lambda s, d=d, fg=fg: s.dma_start(
                out=STR[:].rearrange("p (k q) -> p k q", k=DCH),
                in_=outT_d[d][:, :, fg * 512:(fg + 1) * 512]),
                waits=[lw("str0"), lw("str1")])
            psl = [rot(pz, pz_war, pz_i) for _ in range(4)]
            for k in range(DCH):
                for fi in range(4):
                    i, pb, wwaits = psl[fi]
                    P.op("pe", lambda t, k=k, fi=fi, pb=pb: t.matmul(
                        out=pb[:], lhsT=STR[:, k * 512 + fi * 128:k * 512 + (fi + 1) * 128],
                        rhs=z_sb[:, k * TT:(k + 1) * TT],
                        start=(k == 0), stop=(k == DCH - 1)),
                        waits=([e_w, e_gate] + wwaits) if k == 0 else [])
            e_omm = P.ev("pe")
            last["str0"] = e_omm
            last["str1"] = e_omm
            last["z_used"] = e_omm
            for fi in range(4):
                i, pb, _ = psl[fi]
                fo = fg * 4 + fi
                P.op("act", lambda a, pb=pb: a.activation(
                    out=stage_sb[:], in_=pb[:], func=AF.Copy),
                    waits=[e_omm, lw("stage_used")])
                e_st = P.ev("act")
                pz_war[i] = e_st
                j, pt, wwaits = rot(ptr32, ptr32_war, ptr32_i)
                for q in range(TT // 128):
                    P.op("pe", lambda t, q=q, pt=pt: t.transpose(
                        out=pt[:, q * 128:(q + 1) * 128],
                        in_=stage_sb[:, q * 128:(q + 1) * 128],
                        identity=ident32[:]),
                        waits=([e_st] + wwaits) if q == 0 else [])
                e_tr = P.ev("pe")
                tc0 = t0 // 128
                pcb = piece_sb[0]
                P.op("act", lambda a, pt=pt, pcb=pcb: a.activation(
                    out=pcb[:], in_=pt[:], func=AF.Copy),
                    waits=[e_tr, lw("piece0")])
                ptr32_war[j] = P.ev("act")
                e_pc = ptr32_war[j]
                last["piece0"] = P.dma(
                    "pool", lambda g, d=d, tc0=tc0, fo=fo, pcb=pcb: g.dma_start(
                        out=y_d[d][:, tc0:tc0 + 4, fo * 128:(fo + 1) * 128],
                        in_=pcb[:].rearrange("p (q c) -> p q c", q=4)),
                    waits=[e_pc])
                last["stage_used"] = P.ev("pe")

    e_fwd = mamba_dir("f")
    if e_fwd == "HALT":
        return nc, P, None
    if stop_after == "fwd":
        P.barrier()
        close_scope("mamba")
        return nc, P, dump_sum(nc, P, sb, x_in, y_d, ("f",), out)
    P.barrier()
    e_bwd = mamba_dir("b")
    P.barrier()
    if stop_after == "attn":
        close_scope("mamba")
        return nc, P, dump_sum(nc, P, sb, x_in, y_d, ("f", "b"), out)

    close_scope("mamba")

    acc = sb("acc", [128, NTC * DM], BF16, "ffn")
    hqfm_sb = sb("hqfm_sb", [128, 32 * 4 * 128], BF16, "ffn")
    xqfm_sb = sb("xqfm_sb", [128, 4 * KD * 128], BF16, "ffn")
    hge_sb = sb("hge_sb", [128, 4 * 512], F32, "ffn")
    ffn_sb = sb("ffn_sb", [128, 4 * DM], F32, "ffn")
    fq_bf = sb("fq_bf", [128, 2 * DM], BF16, "ffn")
    ostage = sb("ostage", [128, DM], F32, "ffn")
    bfc_sb = sb("bfc_sb", [1, HID], BF16, "ffn")
    bproj_sb = sb("bproj_sb", [1, DM], BF16, "ffn")
    f_amax = sb("f_amax", [128, 128], F32, "ffn")
    f_e = sb("f_e", [128, 128], F32, "ffn")
    f_scale = sb("f_scale", [128, 128], F32, "ffn")
    f_inv = sb("f_inv", [128, 128], F32, "ffn")
    f_two = sb("f_two", [128, 128], F32, "ffn")
    f_r = sb("f_r", [128, 2048], F32, "ffn")
    f_i = sb("f_i", [128, 1024], I32, "ffn")
    fsum = sb("fsum", [128, DM], F32, "ffn")
    fy1 = sb("fy1", [128, DM], F32, "ffn")
    fy2 = sb("fy2", [128, DM], F32, "ffn")
    ln_m = sb("ln_m", [128, 1], F32, "ffn")
    ln_v = sb("ln_v", [128, 1], F32, "ffn")
    ln_c = sb("ln_c", [128, DM], F32, "ffn")
    sq_dummy = sb("sq_dummy", [128, DM], F32, "ffn")
    QF = {"amax": f_amax, "e": f_e, "scale": f_scale, "inv": f_inv,
          "two": f_two, "r": f_r, "i": f_i}

    e_bprev = None
    for part in range(4):
        e_bl = P.dma("sp", lambda s, part=part: s.dma_start(
            out=fsum[0:1, :], in_=w["bfc"][None, part * DM:(part + 1) * DM]),
            waits=[e_bprev])
        P.op("dve", lambda v, part=part: v.tensor_copy(
            out=bfc_sb[0:1, part * DM:(part + 1) * DM], in_=fsum[0:1, :]), waits=[e_bl])
        e_bprev = P.ev("dve")
    e_bl2 = P.dma("sp", lambda s: s.dma_start(out=fsum[0:1, :], in_=w["bproj"][None, :]),
                  waits=[e_bprev])
    P.op("dve", lambda v: v.tensor_copy(out=bproj_sb[0:1, :], in_=fsum[0:1, :]),
         waits=[e_bl2])
    e_bias = P.ev("dve")
    last["fsum_used"] = e_bias

    for quarter in range(4):
        tcs = [quarter * 4 + i for i in range(4)]
        for qi, tc in enumerate(tcs):
            e_lx = P.dma("sp", lambda s, tc=tc: s.dma_start(
                out=fsum[:], in_=x_in[tc * 128:(tc + 1) * 128, :]), waits=[lw("fsum_used")])
            e_l1 = P.dma("sp", lambda s, tc=tc: s.dma_start(
                out=fy1[:], in_=y_d["f"][:, tc, :]), waits=[lw("fy1_used")])
            e_l2 = P.dma("sp", lambda s, tc=tc: s.dma_start(
                out=fy2[:], in_=y_d["b"][:, tc, :]), waits=[lw("fy2_used")])
            P.op("dve", lambda v: v.tensor_tensor(out=fsum[:], in0=fsum[:], in1=fy1[:],
                                                  op=ALU.add), waits=[e_lx, e_l1])
            P.op("dve", lambda v: v.tensor_tensor(out=fsum[:], in0=fsum[:], in1=fy2[:],
                                                  op=ALU.add), waits=[e_l2])
            e_sum = P.ev("dve")
            last["fy1_used"] = e_sum
            last["fy2_used"] = e_sum
            if stop_after == "ffn0" and quarter == 0 and qi == 0:
                stg = sb("ffdump", [128, DM], F32, "dump")
                e_p = None
                row = 0
                for buf in (fsum, fy1, fy2):
                    P.op("dve", lambda v, buf=buf: v.tensor_copy(out=stg[:], in_=buf[:]),
                         waits=[e_sum] + ([e_p] if e_p else []))
                    e_c2 = P.ev("dve")
                    e_p = P.dma("sp", lambda s, row=row: s.dma_start(
                        out=out[row * 128:(row + 1) * 128, :], in_=stg[:]), waits=[e_c2])
                    row += 1
                P.op("pool", lambda g: g.nop(), waits=[e_p])
                return nc, P, None
            P.op("dve", lambda v: v.tensor_reduce(
                out=ln_m[:], in_=fsum[:], axis=AX.X, op=ALU.add), waits=[lw("lnc_used")])
            e_red = P.ev("dve")
            P.op("dve", lambda v: v.tensor_scalar_mul(out=ln_m[:], in0=ln_m[:], scalar1=1.0 / DM),
                 waits=[e_red])
            e_mean = P.ev("dve")
            P.op("dve", lambda v: v.scalar_tensor_tensor(
                out=ln_c[:], in0=fsum[:], scalar=ln_m[:], in1=fsum[:],
                op0=ALU.subtract, op1=ALU.bypass),
                waits=[e_mean])
            e_c = P.ev("dve")
            P.op("act", lambda a: a.activation(out=sq_dummy[:], in_=ln_c[:], func=AF.Square),
                 waits=[e_c, lw("ln_act"), lw("fsum_used")])
            e_sq = P.ev("act")
            P.op("dve", lambda v: v.tensor_reduce(
                out=ln_v[:], in_=sq_dummy[:], axis=AX.X, op=ALU.add),
                waits=[e_sq, lw("lnv_used")])
            e_vr = P.ev("dve")
            P.op("act", lambda a: a.activation(out=ln_v[:], in_=ln_v[:], func=AF.Ln,
                                               scale=1.0 / DM, bias=eps_sb[:]), waits=[e_vr])
            P.op("act", lambda a: a.activation(out=ln_v[:], in_=ln_v[:], func=AF.Exp, scale=-0.5))
            e_r = P.ev("act")
            last["ln_act"] = e_r
            last["fsum_used"] = e_r
            P.op("dve", lambda v: v.scalar_tensor_tensor(
                out=ln_c[:], in0=ln_c[:], scalar=ln_v[:], in1=ln_c[:],
                op0=ALU.mult, op1=ALU.bypass),
                 waits=[e_r])
            last["lnv_used"] = P.ev("dve")
            P.op("dve", lambda v, tc=tc: v.tensor_copy(
                out=acc[:, tc * DM:(tc + 1) * DM], in_=ln_c[:]))
            e_accw = P.ev("dve")
            if stop_after == "ffnln" and quarter == 0 and qi == 0:
                stg = sb("ffdump", [128, DM], F32, "dump")
                e_p = None
                row = 0
                for buf, nc_ in ((ln_c, DM), (ln_m, 1), (ln_v, 1), (fsum, DM), (fy1, DM)):
                    P.op("dve", lambda v, buf=buf, nc_=nc_: v.tensor_copy(
                        out=stg[:, 0:nc_], in_=buf[:, 0:nc_]),
                        waits=[e_accw] + ([e_p] if e_p else []))
                    e_c2 = P.ev("dve")
                    e_p = P.dma("sp", lambda s, row=row, nc_=nc_: s.dma_start(
                        out=out[row * 128:(row + 1) * 128, 0:nc_], in_=stg[:, 0:nc_]),
                        waits=[e_c2])
                    row += 1
                P.op("pool", lambda g: g.nop(), waits=[e_p])
                return nc, P, None
            e_q = quant_rows(ln_c[:], DM, [e_accw, lw("fq_used")], fq_bf[:, 0:DM], QF)
            last["lnc_used"] = e_q
            if stop_after == "ffnq" and quarter == 0 and qi == 0:
                stg = sb("ffdump", [128, DM], F32, "dump")
                e_p = None
                row = 0
                P.op("dve", lambda v: v.tensor_reduce(
                    out=QF["two"][:, 0:32], in_=ln_c[:].rearrange("p (g k) -> p g k", k=32),
                    axis=AX.X, op=ALU.max, apply_absolute_value=True), waits=[e_q])
                P.ev("dve")
                for buf, nc_ in ((QF["amax"], 32), (QF["e"], 32), (QF["scale"], 32),
                                 (QF["inv"], 32), (QF["r"], DM), (fq_bf, DM), (ln_c, DM),
                                 (QF["two"], 32)):
                    P.op("dve", lambda v, buf=buf, nc_=nc_: v.tensor_copy(
                        out=stg[:, 0:nc_], in_=buf[:, 0:nc_]),
                        waits=[e_q] + ([e_p] if e_p else []))
                    e_c2 = P.ev("dve")
                    e_p = P.dma("sp", lambda s, row=row, nc_=nc_: s.dma_start(
                        out=out[row * 128:(row + 1) * 128, 0:nc_], in_=stg[:, 0:nc_]),
                        waits=[e_c2])
                    row += 1
                P.op("pool", lambda g: g.nop(), waits=[e_p])
                return nc, P, None
            for g in range(KD // 4):
                j, pt, wwaits = rot(ptr, ptr_war, ptr_i)
                for q in range(4):
                    k = g * 4 + q
                    P.op("pe", lambda t, k=k, q=q, pt=pt: t.transpose(
                        out=pt[:, q * 128:(q + 1) * 128], in_=fq_bf[:, k * 128:(k + 1) * 128],
                        identity=ident[:]),
                        waits=([e_q] + wwaits) if q == 0 else [])
                e_tr = P.ev("pe")
                P.op("dve", lambda v, qi=qi, g=g, pt=pt: v.tensor_copy(
                    out=xqfm_sb[:, (qi * KD + g * 4) * 128:(qi * KD + g * 4 + 4) * 128],
                    in_=pt[:]),
                    waits=[e_tr, lw("xqfm_used") if (qi == 0 and g == 0) else None])
                ptr_war[j] = P.ev("dve")
            last["fq_used"] = P.ev("pe")
        e_xq = P.ev("dve")
        if stop_after == "ffn1" and quarter == 0:
            stg = sb("ffdump", [128, DM], F32, "dump")
            e_p = None
            row = 0
            for tc in range(4):
                P.op("dve", lambda v, tc=tc: v.tensor_copy(
                    out=stg[:], in_=acc[:, tc * DM:(tc + 1) * DM]),
                    waits=[e_xq] + ([e_p] if e_p else []))
                e_c2 = P.ev("dve")
                e_p = P.dma("sp", lambda s, row=row: s.dma_start(
                    out=out[row * 128:(row + 1) * 128, :], in_=stg[:]), waits=[e_c2])
                row += 1
            for (buf, nc_) in ((fq_bf, DM), (xqfm_sb, 4 * KD * 128)):
                for j in range(nc_ // DM):
                    P.op("dve", lambda v, buf=buf, j=j: v.tensor_copy(
                        out=stg[:], in_=buf[:, j * DM:(j + 1) * DM]), waits=[e_p])
                    e_c2 = P.ev("dve")
                    e_p = P.dma("sp", lambda s, row=row: s.dma_start(
                        out=out[row * 128:(row + 1) * 128, :], in_=stg[:]), waits=[e_c2])
                    row += 1
            P.op("pool", lambda g: g.nop(), waits=[e_p])
            return nc, P, None

        for f in range(8):
            wb = STR[:, (f % 2) * 4096:(f % 2) * 4096 + 4096]
            e_w = P.dma("sp", lambda s, f=f, wb=wb: s.dma_start(
                out=wb.rearrange("p (k q) -> p k q", k=KD),
                in_=wfcT_d[:, :, f * 512:(f + 1) * 512]),
                waits=[e_prep, lw(f"str{f % 2}")])
            psl = [rot(pz, pz_war, pz_i) for _ in range(4)]
            for k in range(KD):
                for qi in range(4):
                    i, pb, wwaits = psl[qi]
                    P.op("pe", lambda t, k=k, qi=qi, pb=pb, wb=wb: t.matmul(
                        out=pb[:], lhsT=xqfm_sb[:, (qi * KD + k) * 128:(qi * KD + k + 1) * 128],
                        rhs=wb[:, k * 512:(k + 1) * 512], start=(k == 0), stop=False),
                        waits=([e_w, e_xq] + wwaits) if k == 0 else [])
            for qi in range(4):
                i, pb, _ = psl[qi]
                P.op("pe", lambda t, qi=qi, pb=pb, f=f: t.matmul(
                    out=pb[:], lhsT=ones1[:], rhs=bfc_sb[0:1, f * 512:(f + 1) * 512],
                    start=False, stop=True),
                    waits=[e_ones, e_bias] if (f == 0 and qi == 0 and quarter == 0) else [])
            e_fmm = P.ev("pe")
            last[f"str{f % 2}"] = e_fmm
            for qi in range(4):
                i, pb, _ = psl[qi]
                P.op("act", lambda a, qi=qi, pb=pb: a.activation(
                    out=hge_sb[:, qi * 512:(qi + 1) * 512], in_=pb[:], func=AF.Gelu),
                    waits=[e_fmm, lw("hge_used")])
                pz_war[i] = P.ev("act")
            e_ge = P.ev("act")
            e_hq = quant_rows(hge_sb[:, 0:2048], 2048, [e_ge, lw("fq_used")],
                              fq_bf[:, 0:2048], QF)
            last["hge_used"] = e_hq
            for qi in range(4):
                j, pt, wwaits = rot(ptr, ptr_war, ptr_i)
                for jj in range(4):
                    P.op("pe", lambda t, qi=qi, jj=jj, pt=pt: t.transpose(
                        out=pt[:, jj * 128:(jj + 1) * 128],
                        in_=fq_bf[:, qi * 512 + jj * 128:qi * 512 + (jj + 1) * 128],
                        identity=ident[:]),
                        waits=([e_hq] + wwaits) if jj == 0 else [])
                e_tr = P.ev("pe")
                hq4 = hqfm_sb[:].rearrange("p (kk q c) -> p kk q c", q=4, c=128)
                P.op("dve", lambda v, f=f, qi=qi, pt=pt, hq4=hq4: v.tensor_copy(
                    out=hq4[:, f * 4:f * 4 + 4, qi],
                    in_=pt[:].rearrange("p (j c) -> p j c", c=128)),
                    waits=[e_tr, lw("hqfm_used") if (f == 0 and qi == 0) else None])
                ptr_war[j] = P.ev("dve")
            last["fq_used"] = P.ev("pe")
        e_hqf = P.ev("dve")
        if stop_after == "ffn2" and quarter == 0:
            stg = sb("ffdump", [128, DM], F32, "dump")
            e_p = None
            row = 0
            for j in range(2):
                P.op("dve", lambda v, j=j: v.tensor_copy(
                    out=stg[:], in_=hge_sb[:, j * DM:(j + 1) * DM]),
                    waits=[e_hqf] + ([e_p] if e_p else []))
                e_c2 = P.ev("dve")
                e_p = P.dma("sp", lambda s, row=row: s.dma_start(
                    out=out[row * 128:(row + 1) * 128, :], in_=stg[:]), waits=[e_c2])
                row += 1
            for j in range(14):
                P.op("dve", lambda v, j=j: v.tensor_copy(
                    out=stg[:], in_=hqfm_sb[:, j * DM:(j + 1) * DM]), waits=[e_p])
                e_c2 = P.ev("dve")
                e_p = P.dma("sp", lambda s, row=row: s.dma_start(
                    out=out[row * 128:(row + 1) * 128, :], in_=stg[:]), waits=[e_c2])
                row += 1
            P.op("pool", lambda g: g.nop(), waits=[e_p])
            return nc, P, None
        last["xqfm_used"] = e_hqf

        for fo in range(2):
            psl = [rot(pz, pz_war, pz_i) for _ in range(4)]
            for khalf in range(2):
                e_w = P.dma("sp", lambda s, fo=fo, khalf=khalf: s.dma_start(
                    out=STR[:].rearrange("p (k q) -> p k q", k=DCH),
                    in_=wprojT_d[:, khalf * DCH:(khalf + 1) * DCH, fo * 512:(fo + 1) * 512]),
                    waits=[e_prep, lw("str0"), lw("str1")])
                for k16 in range(DCH):
                    k = khalf * DCH + k16
                    for qi in range(4):
                        i, pb, wwaits = psl[qi]
                        P.op("pe", lambda t, k=k, k16=k16, qi=qi, pb=pb: t.matmul(
                            out=pb[:], lhsT=hqfm_sb[:, (k * 4 + qi) * 128:(k * 4 + qi + 1) * 128],
                            rhs=STR[:, k16 * 512:(k16 + 1) * 512], start=(k == 0), stop=False),
                            waits=([e_w, e_hqf] + wwaits) if k == 0 else ([e_w] if k16 == 0 else []))
                e_half = P.ev("pe")
                last["str0"] = e_half
                last["str1"] = e_half
            for qi in range(4):
                i, pb, _ = psl[qi]
                P.op("pe", lambda t, qi=qi, pb=pb, fo=fo: t.matmul(
                    out=pb[:], lhsT=ones1[:], rhs=bproj_sb[0:1, fo * 512:(fo + 1) * 512],
                    start=False, stop=True))
            e_pmm = P.ev("pe")
            last["hqfm_used"] = e_pmm
            for qi in range(4):
                i, pb, _ = psl[qi]
                P.op("act", lambda a, qi=qi, fo=fo, pb=pb: a.activation(
                    out=ffn_sb[:, qi * DM + fo * 512:qi * DM + (fo + 1) * 512], in_=pb[:],
                    func=AF.Copy), waits=[e_pmm, lw("ffn_used")])
                pz_war[i] = P.ev("act")
        e_ffno = P.ev("act")

        e_o = None
        for qi, tc in enumerate(tcs):
            P.op("dve", lambda v, qi=qi, tc=tc: v.tensor_tensor(
                out=ln_c[:], in0=ffn_sb[:, qi * DM:(qi + 1) * DM],
                in1=acc[:, tc * DM:(tc + 1) * DM], op=ALU.add),
                waits=[e_ffno, lw("lnc_used")])
            P.op("dve", lambda v: v.tensor_reduce(out=ln_m[:], in_=ln_c[:], axis=AX.X,
                                                  op=ALU.add))
            e_red = P.ev("dve")
            P.op("dve", lambda v: v.tensor_scalar_mul(out=ln_m[:], in0=ln_m[:], scalar1=1.0 / DM),
                 waits=[e_red])
            e_mean = P.ev("dve")
            P.op("dve", lambda v: v.scalar_tensor_tensor(
                out=ln_c[:], in0=ln_c[:], scalar=ln_m[:], in1=ln_c[:],
                op0=ALU.subtract, op1=ALU.bypass),
                 waits=[e_mean])
            e_c = P.ev("dve")
            P.op("act", lambda a: a.activation(out=sq_dummy[:], in_=ln_c[:], func=AF.Square),
                 waits=[e_c, lw("ln_act"), lw("fsum_used")])
            e_sq = P.ev("act")
            P.op("dve", lambda v: v.tensor_reduce(
                out=ln_v[:], in_=sq_dummy[:], axis=AX.X, op=ALU.add),
                waits=[e_sq, lw("lnv_used")])
            e_vr = P.ev("dve")
            P.op("act", lambda a: a.activation(out=ln_v[:], in_=ln_v[:], func=AF.Ln,
                                               scale=1.0 / DM, bias=eps_sb[:]), waits=[e_vr])
            P.op("act", lambda a: a.activation(out=ln_v[:], in_=ln_v[:], func=AF.Exp, scale=-0.5))
            e_r = P.ev("act")
            last["ln_act"] = e_r
            last["fsum_used"] = e_r
            P.op("dve", lambda v: v.scalar_tensor_tensor(
                out=ostage[:], in0=ln_c[:], scalar=ln_v[:], in1=ln_c[:],
                op0=ALU.mult, op1=ALU.bypass),
                 waits=[e_r, lw("ostage_used")])
            e_o = P.ev("dve")
            last["lnv_used"] = e_o
            last["lnc_used"] = e_o
            e_od = P.dma("sp", lambda s, tc=tc: s.dma_start(
                out=out[tc * 128:(tc + 1) * 128, :], in_=ostage[:]), waits=[e_o])
            last["ostage_used"] = e_od
        last["ffn_used"] = e_o

    P.op("pool", lambda g: g.nop(), waits=[last["ostage_used"]])
    return nc, P, None


def dump_sum(nc, P, sb, x_in, y_d, dirs, out):
    s0 = sb("dump0", [128, DM], F32, "dump")
    s1 = sb("dump1", [128, DM], F32, "dump")
    e_p = None
    e_c = None
    for tc in range(NTC):
        e_lx = P.dma("sp", lambda s, tc=tc: s.dma_start(
            out=s0[:], in_=x_in[tc * 128:(tc + 1) * 128, :]),
            waits=[e_p] if e_p else [])
        waits = [e_lx]
        for d in dirs:
            e_ly = P.dma("sp", lambda s, tc=tc, d=d: s.dma_start(
                out=s1[:], in_=y_d[d][:, tc, :]), waits=[e_c] if e_c else [])
            P.op("dve", lambda v: v.tensor_tensor(out=s0[:], in0=s0[:], in1=s1[:],
                                                  op=ALU.add), waits=waits + [e_ly])
            waits = []
            e_c = P.ev("dve")
        e_p = P.dma("sp", lambda s, tc=tc: s.dma_start(
            out=out[tc * 128:(tc + 1) * 128, :], in_=s0[:]), waits=[e_c])
    P.op("pool", lambda g: g.nop(), waits=[e_p])
    return None


SBUF_LIMIT = 229376 - 16384


def audit_sbuf(nc):
    worst = 0
    for fn in nc.m.functions:
        for a in fn.allocations:
            try:
                locs = a.memorylocations
            except Exception:
                continue
            for loc in locs:
                if "SB" in str(getattr(loc, "type", "")):
                    end = loc.addr + int(np.prod(list(loc.dims)[1:]))
                    if end > worst:
                        worst = end
                    assert end <= SBUF_LIMIT, f"SBUF overflow: {loc.name} ends at {end}"
    return worst


def emit(nc, P):
    audit_sbuf(nc)
    sems = {}
    dsems = {}
    cms = []
    for e in ENGS:
        cm = nc.semaphore(f"sem_{e}")
        sems[e] = cm.__enter__()
        cms.append(cm)
        if P.dma_cnt[e]:
            cm = nc.semaphore(f"dsem_{e}")
            dsems[e] = cm.__enter__()
            cms.append(cm)

    block_cm = nc.Block()
    block = block_cm.__enter__()

    def emit_queue(eng_obj, name):
        for rec in P.q[name]:
            for kind, weng, val in rec["waits"]:
                eng_obj.wait_ge(dsems[weng] if kind == "d" else sems[weng], val)
            inst = rec["fn"](eng_obj)
            if rec["dma"]:
                inst.then_inc(dsems[name], 16)
            elif rec["inc"] is not None:
                eng_obj.drain(fusable=False).then_inc(sems[name], 1)

    @block.sync
    def _(sync):
        emit_queue(sync, "sp")

    @block.tensor
    def _(tensor):
        emit_queue(tensor, "pe")

    @block.scalar
    def _(scalar):
        emit_queue(scalar, "act")

    @block.vector
    def _(vector):
        emit_queue(vector, "dve")

    @block.gpsimd
    def _(gpsimd):
        emit_queue(gpsimd, "pool")

    block_cm.__exit__(None, None, None)
    return sems, dsems


_IN_NAMES = [
    ("x", "L0_attn_x"),
    ("finw", "L0_attn_f_in_w"), ("fconvw", "L0_attn_f_conv_w"), ("fconvb", "L0_attn_f_conv_b"),
    ("fxproj", "L0_attn_f_xproj_w"), ("fdtw", "L0_attn_f_dt_w"), ("fdtb", "L0_attn_f_dt_b"),
    ("falog", "L0_attn_f_A_log"), ("fdpar", "L0_attn_f_D"), ("foutw", "L0_attn_f_out_w"),
    ("binw", "L0_attn_b_in_w"), ("bconvw", "L0_attn_b_conv_w"), ("bconvb", "L0_attn_b_conv_b"),
    ("bxproj", "L0_attn_b_xproj_w"), ("bdtw", "L0_attn_b_dt_w"), ("bdtb", "L0_attn_b_dt_b"),
    ("balog", "L0_attn_b_A_log"), ("bdpar", "L0_attn_b_D"), ("boutw", "L0_attn_b_out_w"),
    ("wfc", "L0_ffn_w_fc"), ("bfc", "L0_ffn_b_fc"), ("wproj", "L0_ffn_w_proj"),
    ("bproj", "L0_ffn_b_proj"),
]


def kernel(**inputs):
    nc, P, _ = build_program("full")
    emit(nc, P)
    xs = np.ascontiguousarray(inputs["L0_attn_x"], dtype=np.float32)
    shared = {short: np.ascontiguousarray(inputs[lname], dtype=np.float32)
              for short, lname in _IN_NAMES if short != "x"}
    in_maps = [dict(shared, x=xs[core]) for core in range(8)]
    res = run_bass_kernel_spmd(nc, in_maps, core_ids=list(range(8)))
    return np.stack([r["out"] for r in res.results], axis=0)


# version 12
# speedup vs baseline: 1.0752x; 1.0752x over previous
import os

import numpy as np

import concourse.bass as bass
import concourse.mybir as mybir
from concourse.bass_utils import run_bass_kernel_spmd

F32 = mybir.dt.float32
BF16 = mybir.dt.bfloat16
I32 = mybir.dt.int32
AF = mybir.ActivationFunctionType
ALU = mybir.AluOpType
AX = mybir.AxisListType

L = 2048
DM = 1024
E = 2048
N = 16
R = 64
HID = 4096
TT = 512
NTILE = L // TT
DCH = E // 128
KD = DM // 128
NTC = L // 128
W96 = R + 2 * N
INV_LN2 = float(1.0 / np.log(2.0))
LN2 = float(np.log(2.0))
TS = TT + 1
MAGIC = float(1.5 * 2.0 ** 23)
LN_EPS = 1e-5

ENGS = ("sp", "pe", "act", "dve", "pool")


class Plan:
    def __init__(self):
        self.q = {e: [] for e in ENGS}
        self.cnt = {e: 0 for e in ENGS}
        self.dma_cnt = {e: 0 for e in ENGS}
        self.pending = {e: [] for e in ENGS}

    def op(self, eng, fn, waits=()):
        ws = [x for x in waits if x is not None] + self.pending[eng]
        self.pending[eng] = []
        rec = {"fn": fn, "waits": ws, "inc": None, "dma": False}
        self.q[eng].append(rec)
        return rec

    def ev(self, eng):
        rec = self.q[eng][-1]
        assert not rec["dma"]
        if rec["inc"] is None:
            self.cnt[eng] += 1
            rec["inc"] = self.cnt[eng]
        return ("e", eng, rec["inc"])

    def dma(self, eng, fn, waits=()):
        ws = [x for x in waits if x is not None] + self.pending[eng]
        self.pending[eng] = []
        rec = {"fn": fn, "waits": ws, "inc": None, "dma": True}
        self.q[eng].append(rec)
        self.dma_cnt[eng] += 16
        rec["inc"] = self.dma_cnt[eng]
        return ("d", eng, rec["inc"])

    def last_event(self, eng):
        if not self.q[eng]:
            return None
        rec = self.q[eng][-1]
        if rec["dma"]:
            return ("d", eng, rec["inc"])
        return self.ev(eng)

    def barrier(self):
        evs = [self.last_event(e) for e in ENGS]
        evs = [x for x in evs if x is not None]
        for e in ENGS:
            own = [x for x in evs if x[1] != e]
            self.pending[e].extend(own)


def build_program(stop_after="full"):
    nc = bass.Bass("TRN2", target_bir_lowering=False, detect_race_conditions=False)
    P = Plan()

    x_in = nc.dram_tensor("x", [L, DM], F32, kind="ExternalInput").ap()
    w = {}
    for d in ("f", "b"):
        w[d + "inw"] = nc.dram_tensor(d + "inw", [2 * E, DM], F32, kind="ExternalInput").ap()
        w[d + "convw"] = nc.dram_tensor(d + "convw", [E, 4], F32, kind="ExternalInput").ap()
        w[d + "convb"] = nc.dram_tensor(d + "convb", [E], F32, kind="ExternalInput").ap()
        w[d + "xproj"] = nc.dram_tensor(d + "xproj", [W96, E], F32, kind="ExternalInput").ap()
        w[d + "dtw"] = nc.dram_tensor(d + "dtw", [E, R], F32, kind="ExternalInput").ap()
        w[d + "dtb"] = nc.dram_tensor(d + "dtb", [E], F32, kind="ExternalInput").ap()
        w[d + "alog"] = nc.dram_tensor(d + "alog", [E, N], F32, kind="ExternalInput").ap()
        w[d + "dpar"] = nc.dram_tensor(d + "dpar", [E], F32, kind="ExternalInput").ap()
        w[d + "outw"] = nc.dram_tensor(d + "outw", [DM, E], F32, kind="ExternalInput").ap()
    w["wfc"] = nc.dram_tensor("wfc", [HID, DM], F32, kind="ExternalInput").ap()
    w["bfc"] = nc.dram_tensor("bfc", [HID], F32, kind="ExternalInput").ap()
    w["wproj"] = nc.dram_tensor("wproj", [DM, HID], F32, kind="ExternalInput").ap()
    w["bproj"] = nc.dram_tensor("bproj", [DM], F32, kind="ExternalInput").ap()
    out = nc.dram_tensor("out", [L, DM], F32, kind="ExternalOutput").ap()

    xfm_d = nc.dram_tensor("xfm_d", [128, KD, L], BF16).ap()
    inT_d = {d: nc.dram_tensor(d + "inT_d", [128, KD, 2 * E], BF16).ap() for d in ("f", "b")}
    outT_d = {d: nc.dram_tensor(d + "outT_d", [128, DCH, DM], BF16).ap() for d in ("f", "b")}
    wfcT_d = nc.dram_tensor("wfcT_d", [128, KD, HID], BF16).ap()
    wprojT_d = nc.dram_tensor("wprojT_d", [128, HID // 128, DM], BF16).ap()
    y_d = {d: nc.dram_tensor(d + "y_d", [128, NTC, DM], F32).ap() for d in ("f", "b")}

    scopes = {}

    def sb(name, shape, dtype, scope="persist"):
        cm = nc.sbuf_tensor(name, shape, dtype)
        t = cm.__enter__()
        scopes.setdefault(scope, []).append(cm)
        return t

    def ps(name, shape, dtype):
        cm = nc.psum_tensor(name, shape, dtype)
        t = cm.__enter__()
        scopes.setdefault("persist", []).append(cm)
        return t

    def close_scope(scope):
        for cm in reversed(scopes.pop(scope, [])):
            cm.__exit__(None, None, None)

    ident = sb("ident", [128, 128], BF16)
    ident32 = sb("ident32", [128, 128], F32)
    sel16 = sb("sel16", [16, N * 128], BF16)
    ones1 = sb("ones1", [1, 128], BF16)
    A_sb = sb("A_sb", [128, 2 * DCH * N], F32)
    D_sb = sb("D_sb", [128, 2 * DCH], F32)
    convw_sb = sb("convw_sb", [128, 2 * DCH * 4], F32)
    convb_sb = sb("convb_sb", [128, 2 * DCH], F32)
    dtb_sb = sb("dtb_sb", [128, 2 * DCH], F32)
    xprojT_sb = sb("xprojT_sb", [128, 2 * DCH * W96], BF16)
    dtwT_sb = sb("dtwT_sb", [64, 2 * E], BF16)
    hcarry = sb("hcarry", [128, DCH * N], F32)
    eps_sb = sb("eps_sb", [128, 1], F32)
    qc_sb = sb("qc_sb", [128, 8], F32)
    QC_VALS = (-0.5 + 1e-5, MAGIC, -MAGIC, 100.0, -103.0, INV_LN2)
    for ci, val in enumerate(QC_VALS):
        nc.const_aps.register(F32, float(val), qc_sb[:, ci:ci + 1])
    STR = sb("STR", [128, 8192], BF16)

    pz = [ps(f"pz{i}", [128, 512], F32) for i in range(4)]
    ptr = [ps(f"ptr{i}", [128, 512], BF16) for i in range(2)]
    ptr32 = [ps(f"ptr32_{i}", [128, 512], F32) for i in range(2)]
    pz_war = [None] * 4
    ptr_war = [None] * 2
    ptr32_war = [None] * 2
    pz_i = [0]
    ptr_i = [0]
    ptr32_i = [0]

    def rot(bufs, wars, idx):
        i = idx[0] % len(bufs)
        idx[0] += 1
        waits = [wars[i]] if wars[i] is not None else []
        return i, bufs[i], waits

    last = {}

    def lw(key):
        return last.get(key)

    bigf_pair = [sb(f"big_f32{i}", [128, 2048], F32, "prep") for i in range(3)]
    bigbf_pair = [sb(f"big_bf{i}", [128, 4096], BF16, "prep") for i in range(3)]
    big_f32 = bigf_pair[0]
    big2_f32 = sb("big2_f32", [128, 2048], F32, "prep")
    big_bf = bigbf_pair[0]
    prep_i = [0]

    def prep_bufs():
        p = prep_i[0] % 3
        prep_i[0] += 1
        return p, bigf_pair[p], bigbf_pair[p]
    qs_amax = sb("qs_amax", [128, 128], F32, "prep")
    qs_e = sb("qs_e", [128, 128], F32, "prep")
    qs_scale = sb("qs_scale", [128, 128], F32, "prep")
    qs_inv = sb("qs_inv", [128, 128], F32, "prep")
    qs_two = sb("qs_two", [128, 128], F32, "prep")
    qs_r = sb("qs_r", [128, 2048], F32, "prep")
    qs_i = sb("qs_i", [128, 2048], I32, "prep")

    def quant_rows(v_ap, ncol, e_ready, out_bf_ap, Q):
        nb = ncol // 32
        P.op("dve", lambda v: v.tensor_reduce(
            out=Q["two"][:, 0:nb], in_=v_ap.rearrange("p (g k) -> p g k", k=32),
            axis=AX.X, op=ALU.max, apply_absolute_value=True),
            waits=list(e_ready))
        e_red = P.ev("dve")
        P.op("dve", lambda v: v.tensor_scalar_max(
            out=Q["amax"][:, 0:nb], in0=Q["two"][:, 0:nb], scalar1=float(2.0 ** -126)),
            waits=[e_red])
        e_am = P.ev("dve")
        P.op("act", lambda a: a.activation(out=Q["e"][:, 0:nb], in_=Q["amax"][:, 0:nb],
                                           func=AF.Ln), waits=[e_am])
        P.op("act", lambda a: a.activation(out=Q["scale"][:, 0:nb], in_=Q["e"][:, 0:nb],
                                           func=AF.Copy, scale=INV_LN2, bias=-0.5 + 1e-5))
        P.op("act", lambda a: a.activation(out=Q["inv"][:, 0:nb], in_=Q["scale"][:, 0:nb],
                                           func=AF.Copy, bias=MAGIC))
        P.op("act", lambda a: a.activation(out=Q["e"][:, 0:nb], in_=Q["inv"][:, 0:nb],
                                           func=AF.Copy, bias=-MAGIC))
        P.op("act", lambda a: a.activation(out=Q["scale"][:, 0:nb], in_=Q["e"][:, 0:nb],
                                           func=AF.Relu, bias=100.0))
        P.op("act", lambda a: a.activation(out=Q["inv"][:, 0:nb], in_=Q["scale"][:, 0:nb],
                                           func=AF.Copy, bias=-103.0))
        P.op("act", lambda a: a.activation(out=Q["amax"][:, 0:nb], in_=Q["inv"][:, 0:nb],
                                           func=AF.Exp, scale=LN2))
        P.op("act", lambda a: a.activation(out=Q["e"][:, 0:nb], in_=Q["inv"][:, 0:nb],
                                           func=AF.Exp, scale=-LN2))
        e_exp = P.ev("act")
        inv_b = Q["e"][:, 0:nb][:, :, None].broadcast_to([128, nb, 32])
        sc_b = Q["amax"][:, 0:nb][:, :, None].broadcast_to([128, nb, 32])
        v3 = v_ap.rearrange("p (g k) -> p g k", k=32)
        r3 = Q["r"][:, 0:ncol].rearrange("p (g k) -> p g k", k=32)
        P.op("dve", lambda v: v.tensor_tensor(out=r3, in0=v3, in1=inv_b, op=ALU.mult),
             waits=[e_exp])
        P.op("dve", lambda v: v.tensor_scalar(
            out=Q["r"][:, 0:ncol], in0=Q["r"][:, 0:ncol], scalar1=MAGIC, scalar2=-MAGIC,
            op0=ALU.add, op1=ALU.add))
        P.op("dve", lambda v: v.tensor_copy(out=Q["i"][:, 0:32], in_=Q["r"][:, 0:32]))
        P.op("dve", lambda v: v.tensor_tensor(
            out=out_bf_ap.rearrange("p (g k) -> p g k", k=32), in0=r3, in1=sc_b, op=ALU.mult))
        return P.ev("dve")

    QP = {"amax": qs_amax, "e": qs_e, "scale": qs_scale, "inv": qs_inv,
          "two": qs_two, "r": qs_r, "i": qs_i}

    P.op("pool", lambda g: g.iota(big_f32[:, 0:128], [[1, 128]], channel_multiplier=-1,
                                  allow_small_or_imprecise_dtypes=True))
    e_i0 = P.ev("pool")
    P.op("dve", lambda v: v.tensor_single_scalar(out=big2_f32[:, 0:128], in_=big_f32[:, 0:128],
                                                 scalar=0.0, op=ALU.is_equal), waits=[e_i0])
    P.op("dve", lambda v: v.tensor_copy(out=ident[:], in_=big2_f32[:, 0:128]))
    P.op("dve", lambda v: v.tensor_copy(out=ident32[:], in_=big2_f32[:, 0:128]))
    e_ident = P.ev("dve")
    P.op("pool", lambda g: g.iota(big_f32[0:16, 0:N * 128], [[1, N], [0, 128]],
                                  channel_multiplier=0, allow_small_or_imprecise_dtypes=True),
         waits=[e_ident])
    P.op("pool", lambda g: g.iota(big2_f32[0:16, 0:N * 128], [[0, N * 128]],
                                  channel_multiplier=1, allow_small_or_imprecise_dtypes=True))
    e_i1 = P.ev("pool")
    P.op("dve", lambda v: v.tensor_tensor(out=big_f32[0:16, 0:N * 128],
                                          in0=big_f32[0:16, 0:N * 128],
                                          in1=big2_f32[0:16, 0:N * 128], op=ALU.is_equal),
         waits=[e_i1])
    P.op("dve", lambda v: v.tensor_copy(out=sel16[:], in_=big_f32[0:16, 0:N * 128]))
    e_sel = P.ev("dve")
    P.op("dve", lambda v: v.memset(ones1[:], 1.0))
    e_ones = P.ev("dve")
    P.op("dve", lambda v: v.memset(eps_sb[:], LN_EPS))
    P.op("dve", lambda v: v.memset(qc_sb[:, 0:1], -0.5 + 1e-5))
    P.op("dve", lambda v: v.memset(qc_sb[:, 1:2], MAGIC))
    P.op("dve", lambda v: v.memset(qc_sb[:, 2:3], -MAGIC))
    P.op("dve", lambda v: v.memset(qc_sb[:, 3:4], 100.0))
    P.op("dve", lambda v: v.memset(qc_sb[:, 4:5], -103.0))
    P.op("dve", lambda v: v.memset(qc_sb[:, 5:6], INV_LN2))
    e_eps = P.ev("dve")

    for di, d in enumerate(("f", "b")):
        o = di * DCH
        P.dma("sp", lambda s, di=di, d=d: s.dma_start(
            out=A_sb[:, di * DCH * N:(di + 1) * DCH * N].rearrange("p (c n) -> p c n", n=N),
            in_=w[d + "alog"].rearrange("(c p) n -> p c n", p=128)))
        if not os.environ.get("NOSLOWDMA"):
            P.dma("sp", lambda s, o=o, d=d: s.dma_start(
                out=D_sb[:, o:o + DCH], in_=w[d + "dpar"].rearrange("(c p) -> p c", p=128),
                allow_slow_non_contiguous=True))
        P.dma("sp", lambda s, di=di, d=d: s.dma_start(
            out=convw_sb[:, di * DCH * 4:(di + 1) * DCH * 4].rearrange("p (c k) -> p c k", k=4),
            in_=w[d + "convw"].rearrange("(c p) k -> p c k", p=128)))
        if not os.environ.get("NOSLOWDMA"):
            P.dma("sp", lambda s, o=o, d=d: s.dma_start(
                out=convb_sb[:, o:o + DCH], in_=w[d + "convb"].rearrange("(c p) -> p c", p=128),
                allow_slow_non_contiguous=True))
        if not os.environ.get("NOSLOWDMA"):
            e_par = P.dma("sp", lambda s, o=o, d=d: s.dma_start(
                out=dtb_sb[:, o:o + DCH], in_=w[d + "dtb"].rearrange("(c p) -> p c", p=128),
                allow_slow_non_contiguous=True))
        else:
            e_par = P.dma("sp", lambda s, o=o, d=d: s.dma_start(
                out=big_f32[0:1, 0:E], in_=w[d + "dtb"][None, :]))
    P.op("act", lambda a: a.activation(out=A_sb[:], in_=A_sb[:], func=AF.Exp), waits=[e_par])
    P.op("dve", lambda v: v.tensor_scalar_mul(out=A_sb[:], in0=A_sb[:], scalar1=-1.0),
         waits=[P.ev("act")])
    e_A = P.ev("dve")

    P.barrier()

    def chunk_transpose_spill(src_row_ap, ncol, dst_ap_fn, quant=False):
        p, bf32, bbf = prep_bufs()
        e_l = P.dma("sp", lambda s, src=src_row_ap, ncol=ncol, bf32=bf32: s.dma_start(
            out=bf32[:, 0:ncol], in_=src), waits=[lw(f"pf{p}")])
        if quant:
            e_q = quant_rows(bf32[:, 0:ncol], ncol,
                             [e_l, lw(f"ptr_rd{p}")], bbf[:, 0:ncol], QP)
        else:
            P.op("dve", lambda v, ncol=ncol, bf32=bf32, bbf=bbf: v.tensor_copy(
                out=bbf[:, 0:ncol], in_=bf32[:, 0:ncol]),
                waits=[e_l, lw(f"ptr_rd{p}")])
            e_q = P.ev("dve")
        last[f"pf{p}"] = e_q
        kb = ncol // 128
        for g in range(kb // 4):
            j, pt, wwaits = rot(ptr, ptr_war, ptr_i)
            for q in range(4):
                k = g * 4 + q
                P.op("pe", lambda t, k=k, q=q, pt=pt, bbf=bbf: t.transpose(
                    out=pt[:, q * 128:(q + 1) * 128], in_=bbf[:, k * 128:(k + 1) * 128],
                    identity=ident[:]),
                    waits=([e_q, e_ident] + wwaits) if q == 0 else [])
            e_tr = P.ev("pe")
            P.op("dve", lambda v, g=g, pt=pt, bbf=bbf: v.tensor_copy(
                out=bbf[:, 2048 + g * 512:2048 + (g + 1) * 512], in_=pt[:]),
                waits=[e_tr] + ([lw(f"pspill{p}")] if g == 0 else []))
            ptr_war[j] = P.ev("dve")
        last[f"ptr_rd{p}"] = P.ev("pe")
        e_s = P.ev("dve")
        last[f"pspill{p}"] = P.dma(
            "pool", lambda s, kb=kb, ncol=ncol, bbf=bbf: s.dma_start(
                out=dst_ap_fn(),
                in_=bbf[:, 2048:2048 + ncol].rearrange("p (k q) -> p k q", k=kb)),
            waits=[e_s])
        return last[f"pspill{p}"]

    for tc in range(NTC):
        chunk_transpose_spill(
            x_in[tc * 128:(tc + 1) * 128, :], DM,
            lambda tc=tc: xfm_d[:, :, tc * 128:(tc + 1) * 128])

    if stop_after == "prepx1":
        stop_after = "prepx"
    else:
      for d in ("f", "b"):
        for fc in range(2 * E // 128):
            chunk_transpose_spill(w[d + "inw"][fc * 128:(fc + 1) * 128, :], DM,
                                  lambda d=d, fc=fc: inT_d[d][:, :, fc * 128:(fc + 1) * 128])
        for fc in range(KD):
            chunk_transpose_spill(w[d + "outw"][fc * 128:(fc + 1) * 128, :], E,
                                  lambda d=d, fc=fc: outT_d[d][:, :, fc * 128:(fc + 1) * 128])
        di = 0 if d == "f" else 1
        p, bf32, bbf = prep_bufs()
        e_l = P.dma("sp", lambda s, d=d, bf32=bf32: s.dma_start(
            out=bf32[0:96, 0:E], in_=w[d + "xproj"]), waits=[lw(f"pf{p}")])
        P.op("dve", lambda v, bf32=bf32, bbf=bbf: v.tensor_copy(
            out=bbf[0:96, 0:E], in_=bf32[0:96, 0:E]),
            waits=[e_l, lw(f"ptr_rd{p}")])
        e_c = P.ev("dve")
        last[f"pf{p}"] = e_c
        for g in range(DCH // 4):
            j, pt, wwaits = rot(ptr, ptr_war, ptr_i)
            for q in range(4):
                c = g * 4 + q
                P.op("pe", lambda t, c=c, q=q, pt=pt, bbf=bbf: t.transpose(
                    out=pt[0:128, q * 96:q * 96 + 96], in_=bbf[0:96, c * 128:(c + 1) * 128],
                    identity=ident[0:96, 0:96]),
                    waits=([e_c, e_ident] + wwaits) if q == 0 else [])
            e_tr = P.ev("pe")
            P.op("dve", lambda v, g=g, pt=pt, di=di: v.tensor_copy(
                out=xprojT_sb[:, (di * DCH + g * 4) * W96:(di * DCH + g * 4 + 4) * W96],
                in_=pt[0:128, 0:4 * 96]), waits=[e_tr])
            ptr_war[j] = P.ev("dve")
        last[f"ptr_rd{p}"] = P.ev("pe")
        p, bf32, bbf = prep_bufs()
        e_l = P.dma("sp", lambda s, d=d, bf32=bf32: s.dma_start(
            out=bf32[:, 0:DCH * R].rearrange("p (c r) -> p c r", r=R),
            in_=w[d + "dtw"].rearrange("(c p) r -> p c r", p=128)),
            waits=[lw(f"pf{p}")])
        P.op("dve", lambda v, bf32=bf32, bbf=bbf: v.tensor_copy(
            out=bbf[:, 0:DCH * R], in_=bf32[:, 0:DCH * R]),
            waits=[e_l, lw(f"ptr_rd{p}")])
        e_c = P.ev("dve")
        last[f"pf{p}"] = e_c
        for g in range(DCH // 4):
            j, pt, wwaits = rot(ptr, ptr_war, ptr_i)
            for q in range(4):
                c = g * 4 + q
                P.op("pe", lambda t, c=c, q=q, pt=pt, bbf=bbf: t.transpose(
                    out=pt[0:64, q * 128:(q + 1) * 128], in_=bbf[:, c * R:(c + 1) * R],
                    identity=ident[:]),
                    waits=([e_c, e_ident] + wwaits) if q == 0 else [])
            e_tr = P.ev("pe")
            P.op("dve", lambda v, g=g, pt=pt, di=di: v.tensor_copy(
                out=dtwT_sb[0:64, di * E + g * 512:di * E + (g + 1) * 512],
                in_=pt[0:64, 0:512]), waits=[e_tr])
            ptr_war[j] = P.ev("dve")
        last[f"ptr_rd{p}"] = P.ev("pe")

    if stop_after != "prepx":
        for fc in range(HID // 128):
            chunk_transpose_spill(w["wfc"][fc * 128:(fc + 1) * 128, :], DM,
                                  lambda fc=fc: wfcT_d[:, :, fc * 128:(fc + 1) * 128], quant=True)
        for fc in range(DM // 128):
            for half in range(2):
                chunk_transpose_spill(
                    w["wproj"][fc * 128:(fc + 1) * 128, half * 2048:(half + 1) * 2048], 2048,
                    lambda fc=fc, half=half: wprojT_d[:, half * DCH:(half + 1) * DCH,
                                                      fc * 128:(fc + 1) * 128], quant=True)
    e_prep = None

    if stop_after in ("prepw", "prepx"):
        srcten = inT_d["f"] if stop_after == "prepw" else xfm_d
        P.barrier()
        stg_bf = sb("dstg_bf", [128, DM], BF16, "dump")
        stg_f = sb("dstg_f", [128, DM], F32, "dump")
        e_p = None
        e_c2 = None
        for j in range(16):
            Kn = KD
            kk = j % Kn
            c0 = (j // Kn) * DM
            e_l = P.dma("sp", lambda s, kk=kk, c0=c0, srcten=srcten: s.dma_start(
                out=stg_bf[:], in_=srcten[:, kk, c0:c0 + DM]), waits=[e_p])
            P.op("dve", lambda v: v.tensor_copy(out=stg_f[:], in_=stg_bf[:]),
                 waits=[e_l] + ([e_c2] if e_c2 else []))
            e_c2 = P.ev("dve")
            e_p = P.dma("sp", lambda s, j=j: s.dma_start(
                out=out[j * 128:(j + 1) * 128, :], in_=stg_f[:]), waits=[e_c2])
        P.op("pool", lambda g: g.nop(), waits=[e_p])
        return nc, P, None

    close_scope("prep")
    P.barrier()

    xh_sb = sb("xh_sb", [128, DCH * TT], BF16, "mamba")
    halo_sb = sb("halo_sb", [128, DCH * 4], BF16, "mamba")
    xc_sb = sb("xc_sb", [128, DCH * TT], BF16, "mamba")
    z_sb = sb("z_sb", [128, DCH * TT], BF16, "mamba")
    bball_sb = sb("bball_sb", [128, N * TT], BF16, "mamba")
    cball_sb = sb("cball_sb", [128, N * TT], BF16, "mamba")
    dt_sb = sb("dt_sb", [64, TT], BF16, "mamba")
    delta_sb = [sb(f"delta{i}", [128, TT], F32, "mamba") for i in range(2)]
    spe_sb = sb("spe", [128, TT], F32, "mamba")
    dAall_sb = sb("dAall", [128, N * TS], BF16, "mamba")
    cacc_sb = sb("cacc", [128, 2 * TT], F32, "mamba")
    dxc_sb = sb("dxc", [128, TT], BF16, "mamba")
    dbuall_sb = sb("dbuall", [128, N * TS], BF16, "mamba")
    bc_rows = sb("bc_rows", [16, 2 * TT], BF16, "mamba")
    ysc_sb = [sb(f"ysc{i}", [128, TT], BF16, "mamba") for i in range(2)]
    y2_sb = sb("y2", [128, TT], BF16, "mamba")
    sz_sb = sb("sz", [128, TT], BF16, "mamba")
    stage_sb = sb("stage_sb", [128, TT], F32, "mamba")
    piece_sb = [sb("piece0", [128, 512], F32, "mamba")]
    xfmt_sb = sb("xfmt_sb", [128, KD * TT], BF16, "mamba")


    def dump_fm(bufs, waits):
        stg = cacc_sb
        row = 0
        e_p = None
        ws = list(waits)
        for buf, ncol in bufs:
            for j in range(ncol // DM):
                P.op("dve", lambda v, buf=buf, j=j: v.tensor_copy(
                    out=stg[:], in_=buf[:, j * DM:(j + 1) * DM]),
                    waits=ws + ([e_p] if e_p else []))
                ws = []
                e_c = P.ev("dve")
                e_p = P.dma("sp", lambda s, row=row: s.dma_start(
                    out=out[row * 128:(row + 1) * 128, :], in_=stg[:]), waits=[e_c])
                row += 1
        P.op("pool", lambda g: g.nop(), waits=[e_p])
        return "HALT"

    pending_out = [None]

    def mamba_dir(d):
        di = 0 if d == "f" else 1
        rev = d == "b"
        tiles = list(range(NTILE - 1, -1, -1)) if rev else list(range(NTILE))

        P.op("dve", lambda v: v.memset(hcarry[:], 0.0))
        P.op("dve", lambda v: v.memset(halo_sb[:], 0.0))
        P.op("dve", lambda v: v.memset(
            dAall_sb[:].rearrange("p (n t) -> p n t", n=N)[:, :, 0], 0.0))
        e_init = P.ev("dve")
        e_yg = e_init

        for ti, tile in enumerate(tiles):
            t0 = tile * TT
            e_xf = P.dma("sp", lambda g, t0=t0: g.dma_start(
                out=xfmt_sb[:].rearrange("p (k q) -> p k q", k=KD),
                in_=xfm_d[:, :, t0:t0 + TT]),
                waits=[e_prep, lw("xfmt")])

            for fg in range(8):
                if fg == 4 and pending_out[0] is not None:
                    t0p, e_gp = pending_out[0]
                    pending_out[0] = None
                    emit_out_proj(d, t0p, e_gp)
                wb = STR[:, (fg % 2) * 4096:(fg % 2) * 4096 + 4096]
                e_w = P.dma("sp", lambda s, d=d, fg=fg, wb=wb: s.dma_start(
                    out=wb.rearrange("p (k q) -> p k q", k=KD),
                    in_=inT_d[d][:, :, fg * 512:(fg + 1) * 512]),
                    waits=[e_prep, lw(f"str{fg % 2}")])
                psl = [rot(pz, pz_war, pz_i) for _ in range(4)]
                for k in range(KD):
                    for fi in range(4):
                        i, pb, wwaits = psl[fi]
                        P.op("pe", lambda t, wb=wb, pb=pb, fi=fi, k=k: t.matmul(
                            out=pb[:], lhsT=wb[:, k * 512 + fi * 128:k * 512 + (fi + 1) * 128],
                            rhs=xfmt_sb[:, k * TT:(k + 1) * TT],
                            start=(k == 0), stop=(k == KD - 1)),
                            waits=([e_w, e_xf] + wwaits) if k == 0 else [])
                e_mm = P.ev("pe")
                last[f"str{fg % 2}"] = e_mm
                last["xfmt"] = e_mm
                for fi in range(4):
                    i, pb, _ = psl[fi]
                    fo = fg * 4 + fi
                    if fo < DCH:
                        c = fo
                        first = [e_mm, lw("xh_used"), lw("tree_war")] if fi == 0 else []
                        if rev:
                            P.op("dve", lambda v, c=c, pb=pb: v.tensor_copy(
                                out=xh_sb[:, c * TT:(c + 1) * TT][:, ::-1], in_=pb[:]),
                                waits=first)
                        else:
                            P.op("act", lambda a, c=c, pb=pb: a.activation(
                                out=xh_sb[:, c * TT:(c + 1) * TT], in_=pb[:], func=AF.Copy),
                                waits=first)
                    else:
                        c = fo - DCH
                        P.op("act", lambda a, c=c, pb=pb: a.activation(
                            out=z_sb[:, c * TT:(c + 1) * TT], in_=pb[:], func=AF.Copy),
                            waits=[e_mm, lw("z_used")] if fi == 0 else [])
                e_ev = P.ev("dve" if (rev and fg < 4) else "act")
                for fi in range(4):
                    pz_war[psl[fi][0]] = e_ev
            e_xh_a = P.ev("act")
            e_xh_d = P.ev("dve") if rev else None
            e_z = e_xh_a
            if stop_after == "xh0" and ti == 0:
                return dump_fm([(xh_sb, DCH * TT), (z_sb, DCH * TT)], [e_xh_a, e_xh_d])

            for c in range(DCH):
                cacc = cacc_sb[:, (c % 2) * TT:(c % 2 + 1) * TT]
                wcol = lambda k, c=c: convw_sb[:, (di * DCH + c) * 4 + k:(di * DCH + c) * 4 + k + 1]
                halo_c = halo_sb[:, c * 4:c * 4 + 3]
                xh_c = xh_sb[:, c * TT:(c + 1) * TT]
                P.op("dve", lambda v, xh_c=xh_c, cacc=cacc, wcol=wcol: v.tensor_scalar_mul(
                    out=cacc[:], in0=xh_c, scalar1=wcol(3)),
                    waits=[e_xh_a, e_xh_d, lw(f"cacc{c % 2}")])
                for k in range(3):
                    nh = 3 - k
                    P.op("dve", lambda v, k=k, nh=nh, halo_c=halo_c, cacc=cacc, wcol=wcol:
                         v.scalar_tensor_tensor(
                             out=cacc[:, 0:nh], in0=halo_c[:, k:3], scalar=wcol(k),
                             in1=cacc[:, 0:nh], op0=ALU.mult, op1=ALU.add))
                    P.op("dve", lambda v, k=k, nh=nh, xh_c=xh_c, cacc=cacc, wcol=wcol:
                         v.scalar_tensor_tensor(
                             out=cacc[:, nh:TT], in0=xh_c[:, 0:TT - nh], scalar=wcol(k),
                             in1=cacc[:, nh:TT], op0=ALU.mult, op1=ALU.add))
                e_cv = P.ev("dve")
                P.op("dve", lambda v, c=c, xh_c=xh_c: v.tensor_copy(
                    out=halo_sb[:, c * 4:c * 4 + 3], in_=xh_c[:, TT - 3:TT]))
                P.op("act", lambda a, c=c, cacc=cacc, di=di: a.activation(
                    out=xc_sb[:, c * TT:(c + 1) * TT], in_=cacc[:], func=AF.Silu,
                    bias=convb_sb[:, di * DCH + c:di * DCH + c + 1]),
                    waits=[e_cv, lw(f"xcg{c}")])
                last[f"cacc{c % 2}"] = P.ev("act")
            e_xc = P.ev("act")
            if stop_after == "xc0" and ti == 0:
                return dump_fm([(xc_sb, DCH * TT)], [e_xc])

            groups = [(0, R, 64), (R, R + N, 16), (R + N, W96, 16)]
            e_gmm = []
            pgrp = []
            for gi, (c0, c1, rows) in enumerate(groups):
                i, pb, wwaits = rot(pz, pz_war, pz_i)
                pgrp.append((i, pb))
                for c in range(DCH):
                    P.op("pe", lambda t, c=c, c0=c0, c1=c1, rows=rows, pb=pb, di=di: t.matmul(
                        out=pb[0:rows, :],
                        lhsT=xprojT_sb[:, (di * DCH + c) * W96 + c0:(di * DCH + c) * W96 + c1],
                        rhs=xc_sb[:, c * TT:(c + 1) * TT],
                        start=(c == 0), stop=(c == DCH - 1)),
                        waits=([e_xc] + wwaits) if c == 0 else [])
                e_gmm.append(P.ev("pe"))
            i_dt, pb_dt = pgrp[0]
            P.op("act", lambda a, pb=pb_dt: a.activation(out=dt_sb[0:64, :], in_=pb[0:64, :],
                                                         func=AF.Copy),
                 waits=[e_gmm[0], lw("dt_used")])
            e_dt = P.ev("act")
            pz_war[i_dt] = e_dt
            iB, pbB = pgrp[1]
            iC, pbC = pgrp[2]
            P.op("act", lambda a, pb=pbB: a.activation(out=bc_rows[0:16, 0:TT], in_=pb[0:16, :],
                                                       func=AF.Copy),
                 waits=[e_gmm[1], lw("bcsrc_used")])
            pz_war[iB] = P.ev("act")
            P.op("act", lambda a, pb=pbC: a.activation(out=bc_rows[0:16, TT:2 * TT],
                                                       in_=pb[0:16, :], func=AF.Copy),
                 waits=[e_gmm[2]])
            pz_war[iC] = P.ev("act")
            e_bcsrc = P.ev("act")

            bc_jobs = [(dst, off, n) for n in range(N)
                       for (dst, off) in ((bball_sb, 0), (cball_sb, TT))]
            for g0 in range(0, len(bc_jobs), 4):
                grp = []
                for gi, (dst, off, n) in enumerate(bc_jobs[g0:g0 + 4]):
                    i, pb, wwaits = rot(pz, pz_war, pz_i)
                    grp.append((i, pb, dst, n))
                    P.op("pe", lambda t, n=n, pb=pb, off=off: t.matmul(
                        out=pb[:], lhsT=sel16[:, n * 128:(n + 1) * 128],
                        rhs=bc_rows[0:16, off:off + TT], start=True, stop=True),
                        waits=[e_bcsrc if g0 == 0 and gi == 0 else None, e_sel] + wwaits)
                e_bm = P.ev("pe")
                for ci, (i, pb, dst, n) in enumerate(grp):
                    P.op("dve", lambda v, n=n, pb=pb, dst=dst: v.tensor_copy(
                        out=dst[:, n * TT:(n + 1) * TT], in_=pb[:]),
                        waits=[e_bm if ci == 0 else None])
                e_g = P.ev("dve")
                for (i, pb, dst, n) in grp:
                    pz_war[i] = e_g
            e_bcall = P.ev("dve")
            last["bcsrc_used"] = P.ev("pe")
            if stop_after == "bc0" and ti == 0:
                return dump_fm([(bball_sb, N * TT), (cball_sb, N * TT), (dt_sb, 0)],
                               [e_bcall, e_dt])

            dA3 = dAall_sb[:].rearrange("p (n t) -> p n t", n=N)
            dbu3 = dbuall_sb[:].rearrange("p (n t) -> p n t", n=N)
            hx3 = xh_sb[:].rearrange("p (n t) -> p n t", n=N)

            def emit_gate(c, e_l1):
                P.op("dve", lambda v, hx3=hx3: v.tensor_tensor(
                    out=hx3[:, 0:4], in0=hx3[:, 0:4], in1=hx3[:, 4:8], op=ALU.add),
                    waits=[e_l1])
                P.op("dve", lambda v, hx3=hx3: v.tensor_tensor(
                    out=hx3[:, 0:2], in0=hx3[:, 0:2], in1=hx3[:, 2:4], op=ALU.add))
                P.op("dve", lambda v, c=c, hx3=hx3: v.tensor_tensor(
                    out=ysc_sb[c % 2][:], in0=hx3[:, 0], in1=hx3[:, 1], op=ALU.add))
                P.op("dve", lambda v, c=c, di=di: v.scalar_tensor_tensor(
                    out=y2_sb[:], in0=xc_sb[:, c * TT:(c + 1) * TT],
                    scalar=D_sb[:, di * DCH + c:di * DCH + c + 1], in1=ysc_sb[c % 2][:],
                    op0=ALU.mult, op1=ALU.add))
                P.op("dve", lambda v, c=c, rev=rev: v.tensor_tensor(
                    out=z_sb[:, c * TT:(c + 1) * TT],
                    in0=(y2_sb[:, ::-1] if rev else y2_sb[:]), in1=sz_sb[:], op=ALU.mult),
                    waits=[lw("sz_ev")])
                e_yg = P.ev("dve")
                last["sz_used"] = e_yg
                last[f"xcg{c}"] = e_yg
                return e_yg

            for c in range(DCH):
                dbuf = delta_sb[c % 2]
                i, pb, wwaits = rot(pz, pz_war, pz_i)
                P.op("pe", lambda t, c=c, pb=pb, di=di: t.matmul(
                    out=pb[:], lhsT=dtwT_sb[0:64, di * E + c * 128:di * E + (c + 1) * 128],
                    rhs=dt_sb[0:64, :], start=True, stop=True),
                    waits=[e_dt, lw(f"delta_used{c % 2}")] + wwaits)
                e_dmm = P.ev("pe")
                if c == DCH - 1:
                    last["dt_used"] = e_dmm
                P.op("act", lambda a, c=c, pb=pb, di=di: a.activation(
                    out=spe_sb[:], in_=pb[:], func=AF.Exp,
                    bias=dtb_sb[:, di * DCH + c:di * DCH + c + 1]),
                    waits=[e_dmm])
                pz_war[i] = P.ev("act")
                P.op("act", lambda a, dbuf=dbuf: a.activation(
                    out=dbuf[:], in_=spe_sb[:], func=AF.Ln, bias=1.0))
                e_delta = P.ev("act")
                if c > 0:
                    P.op("act", lambda a, c=c: a.activation(
                        out=sz_sb[:], in_=z_sb[:, (c - 1) * TT:c * TT], func=AF.Silu),
                        waits=[e_z, lw("sz_used")])
                    last["sz_ev"] = P.ev("act")
                P.op("dve", lambda v, c=c, dbuf=dbuf: v.tensor_tensor(
                    out=dxc_sb[:], in0=dbuf[:], in1=xc_sb[:, c * TT:(c + 1) * TT], op=ALU.mult),
                    waits=[e_delta])
                P.op("dve", lambda v, c=c: v.tensor_copy(
                    out=dbu3[:, :, 0], in_=hcarry[:, c * N:(c + 1) * N]))
                P.op("dve", lambda v: v.tensor_tensor(
                    out=dbu3[:, :, 1:TS],
                    in0=dxc_sb[:][:, None, :].broadcast_to([128, N, TT]),
                    in1=bball_sb[:].rearrange("p (n t) -> p n t", n=N), op=ALU.mult),
                    waits=[e_bcall if c == 0 else None])
                e_dAh = []
                for h in range(2):
                    for j in range(8):
                        n = h * 8 + j
                        P.op("act", lambda a, c=c, n=n, dbuf=dbuf, di=di: a.activation(
                            out=dA3[:, n, 1:TS], in_=dbuf[:], func=AF.Exp,
                            scale=A_sb[:, (di * DCH + c) * N + n:(di * DCH + c) * N + n + 1]),
                            waits=[lw(f"dA_war{h}") if j == 0 else None])
                    e_dAh.append(P.ev("act"))
                for h in range(2):
                    o0 = h * 8 * TS
                    P.op("dve", lambda v, o0=o0: v.tensor_tensor_scan(
                        out=dbuall_sb[:, o0:o0 + 8 * TS], data0=dAall_sb[:, o0:o0 + 8 * TS],
                        data1=dbuall_sb[:, o0:o0 + 8 * TS], initial=0.0,
                        op0=ALU.mult, op1=ALU.add),
                        waits=[e_dAh[h]])
                    last[f"dA_war{h}"] = P.ev("dve")
                e_scan = lw("dA_war1")
                last[f"delta_used{c % 2}"] = e_scan
                P.op("dve", lambda v, c=c: v.tensor_copy(
                    out=hcarry[:, c * N:(c + 1) * N], in_=dbu3[:, :, TS - 1]))
                if c > 0:
                    emit_gate(c - 1, e_l1_prev)
                P.op("dve", lambda v, hx3=hx3: v.tensor_tensor(
                    out=hx3[:], in0=cball_sb[:].rearrange("p (n t) -> p n t", n=N),
                    in1=dbu3[:, :, 1:TS], op=ALU.mult))
                e_hm = P.ev("dve")
                P.op("pool", lambda g, hx3=hx3: g.tensor_tensor(
                    out=hx3[:, 0:8], in0=hx3[:, 0:8],
                    in1=hx3[:, 8:16], op=ALU.add),
                    waits=[e_hm])
                e_l1_prev = P.ev("pool")
                last["tree_war"] = e_l1_prev
            P.op("act", lambda a: a.activation(
                out=sz_sb[:], in_=z_sb[:, (DCH - 1) * TT:DCH * TT], func=AF.Silu),
                waits=[e_z, lw("sz_used")])
            last["sz_ev"] = P.ev("act")
            e_gate = emit_gate(DCH - 1, e_l1_prev)
            if stop_after == "y0" and ti == 0:
                return dump_fm([(z_sb, DCH * TT)], [e_gate])
            last["xh_used"] = e_gate
            pending_out[0] = (t0, e_gate)
        t0p, e_gp = pending_out[0]
        pending_out[0] = None
        emit_out_proj(d, t0p, e_gp)
        return None

    def emit_out_proj(d, t0, e_gate):
        for fg in range(2):
            e_w = P.dma("sp", lambda s, d=d, fg=fg: s.dma_start(
                out=STR[:].rearrange("p (k q) -> p k q", k=DCH),
                in_=outT_d[d][:, :, fg * 512:(fg + 1) * 512]),
                waits=[lw("str0"), lw("str1")])
            psl = [rot(pz, pz_war, pz_i) for _ in range(4)]
            for k in range(DCH):
                for fi in range(4):
                    i, pb, wwaits = psl[fi]
                    P.op("pe", lambda t, k=k, fi=fi, pb=pb: t.matmul(
                        out=pb[:], lhsT=STR[:, k * 512 + fi * 128:k * 512 + (fi + 1) * 128],
                        rhs=z_sb[:, k * TT:(k + 1) * TT],
                        start=(k == 0), stop=(k == DCH - 1)),
                        waits=([e_w, e_gate] + wwaits) if k == 0 else [])
            e_omm = P.ev("pe")
            last["str0"] = e_omm
            last["str1"] = e_omm
            last["z_used"] = e_omm
            for fi in range(4):
                i, pb, _ = psl[fi]
                fo = fg * 4 + fi
                P.op("act", lambda a, pb=pb: a.activation(
                    out=stage_sb[:], in_=pb[:], func=AF.Copy),
                    waits=[e_omm, lw("stage_used")])
                e_st = P.ev("act")
                pz_war[i] = e_st
                j, pt, wwaits = rot(ptr32, ptr32_war, ptr32_i)
                for q in range(TT // 128):
                    P.op("pe", lambda t, q=q, pt=pt: t.transpose(
                        out=pt[:, q * 128:(q + 1) * 128],
                        in_=stage_sb[:, q * 128:(q + 1) * 128],
                        identity=ident32[:]),
                        waits=([e_st] + wwaits) if q == 0 else [])
                e_tr = P.ev("pe")
                tc0 = t0 // 128
                pcb = piece_sb[0]
                P.op("act", lambda a, pt=pt, pcb=pcb: a.activation(
                    out=pcb[:], in_=pt[:], func=AF.Copy),
                    waits=[e_tr, lw("piece0")])
                ptr32_war[j] = P.ev("act")
                e_pc = ptr32_war[j]
                last["piece0"] = P.dma(
                    "pool", lambda g, d=d, tc0=tc0, fo=fo, pcb=pcb: g.dma_start(
                        out=y_d[d][:, tc0:tc0 + 4, fo * 128:(fo + 1) * 128],
                        in_=pcb[:].rearrange("p (q c) -> p q c", q=4)),
                    waits=[e_pc])
                last["stage_used"] = P.ev("pe")

    e_fwd = mamba_dir("f")
    if e_fwd == "HALT":
        return nc, P, None
    if stop_after == "fwd":
        P.barrier()
        close_scope("mamba")
        return nc, P, dump_sum(nc, P, sb, x_in, y_d, ("f",), out)
    P.barrier()
    e_bwd = mamba_dir("b")
    P.barrier()
    if stop_after == "attn":
        close_scope("mamba")
        return nc, P, dump_sum(nc, P, sb, x_in, y_d, ("f", "b"), out)

    close_scope("mamba")

    acc = sb("acc", [128, NTC * DM], BF16, "ffn")
    hqfm_sb = sb("hqfm_sb", [128, 32 * 4 * 128], BF16, "ffn")
    xqfm_sb = sb("xqfm_sb", [128, 4 * KD * 128], BF16, "ffn")
    hge_sb = sb("hge_sb", [128, 4 * 512], F32, "ffn")
    ffn_sb = sb("ffn_sb", [128, 4 * DM], F32, "ffn")
    fq_bf = sb("fq_bf", [128, 2 * DM], BF16, "ffn")
    ostage = sb("ostage", [128, DM], F32, "ffn")
    bfc_sb = sb("bfc_sb", [1, HID], BF16, "ffn")
    bproj_sb = sb("bproj_sb", [1, DM], BF16, "ffn")
    f_amax = sb("f_amax", [128, 128], F32, "ffn")
    f_e = sb("f_e", [128, 128], F32, "ffn")
    f_scale = sb("f_scale", [128, 128], F32, "ffn")
    f_inv = sb("f_inv", [128, 128], F32, "ffn")
    f_two = sb("f_two", [128, 128], F32, "ffn")
    f_r = sb("f_r", [128, 2048], F32, "ffn")
    f_i = sb("f_i", [128, 1024], I32, "ffn")
    fsum = sb("fsum", [128, DM], F32, "ffn")
    fy1 = sb("fy1", [128, DM], F32, "ffn")
    fy2 = sb("fy2", [128, DM], F32, "ffn")
    ln_m = sb("ln_m", [128, 1], F32, "ffn")
    ln_v = sb("ln_v", [128, 1], F32, "ffn")
    ln_c = sb("ln_c", [128, DM], F32, "ffn")
    sq_dummy = sb("sq_dummy", [128, DM], F32, "ffn")
    QF = {"amax": f_amax, "e": f_e, "scale": f_scale, "inv": f_inv,
          "two": f_two, "r": f_r, "i": f_i}

    e_bprev = None
    for part in range(4):
        e_bl = P.dma("sp", lambda s, part=part: s.dma_start(
            out=fsum[0:1, :], in_=w["bfc"][None, part * DM:(part + 1) * DM]),
            waits=[e_bprev])
        P.op("dve", lambda v, part=part: v.tensor_copy(
            out=bfc_sb[0:1, part * DM:(part + 1) * DM], in_=fsum[0:1, :]), waits=[e_bl])
        e_bprev = P.ev("dve")
    e_bl2 = P.dma("sp", lambda s: s.dma_start(out=fsum[0:1, :], in_=w["bproj"][None, :]),
                  waits=[e_bprev])
    P.op("dve", lambda v: v.tensor_copy(out=bproj_sb[0:1, :], in_=fsum[0:1, :]),
         waits=[e_bl2])
    e_bias = P.ev("dve")
    last["fsum_used"] = e_bias

    for quarter in range(4):
        tcs = [quarter * 4 + i for i in range(4)]
        for qi, tc in enumerate(tcs):
            e_lx = P.dma("sp", lambda s, tc=tc: s.dma_start(
                out=fsum[:], in_=x_in[tc * 128:(tc + 1) * 128, :]), waits=[lw("fsum_used")])
            e_l1 = P.dma("sp", lambda s, tc=tc: s.dma_start(
                out=fy1[:], in_=y_d["f"][:, tc, :]), waits=[lw("fy1_used")])
            e_l2 = P.dma("sp", lambda s, tc=tc: s.dma_start(
                out=fy2[:], in_=y_d["b"][:, tc, :]), waits=[lw("fy2_used")])
            P.op("dve", lambda v: v.tensor_tensor(out=fsum[:], in0=fsum[:], in1=fy1[:],
                                                  op=ALU.add), waits=[e_lx, e_l1])
            P.op("dve", lambda v: v.tensor_tensor(out=fsum[:], in0=fsum[:], in1=fy2[:],
                                                  op=ALU.add), waits=[e_l2])
            e_sum = P.ev("dve")
            last["fy1_used"] = e_sum
            last["fy2_used"] = e_sum
            if stop_after == "ffn0" and quarter == 0 and qi == 0:
                stg = sb("ffdump", [128, DM], F32, "dump")
                e_p = None
                row = 0
                for buf in (fsum, fy1, fy2):
                    P.op("dve", lambda v, buf=buf: v.tensor_copy(out=stg[:], in_=buf[:]),
                         waits=[e_sum] + ([e_p] if e_p else []))
                    e_c2 = P.ev("dve")
                    e_p = P.dma("sp", lambda s, row=row: s.dma_start(
                        out=out[row * 128:(row + 1) * 128, :], in_=stg[:]), waits=[e_c2])
                    row += 1
                P.op("pool", lambda g: g.nop(), waits=[e_p])
                return nc, P, None
            P.op("dve", lambda v: v.tensor_reduce(
                out=ln_m[:], in_=fsum[:], axis=AX.X, op=ALU.add), waits=[lw("lnc_used")])
            e_red = P.ev("dve")
            P.op("dve", lambda v: v.tensor_scalar_mul(out=ln_m[:], in0=ln_m[:], scalar1=1.0 / DM),
                 waits=[e_red])
            e_mean = P.ev("dve")
            P.op("dve", lambda v: v.scalar_tensor_tensor(
                out=ln_c[:], in0=fsum[:], scalar=ln_m[:], in1=fsum[:],
                op0=ALU.subtract, op1=ALU.bypass),
                waits=[e_mean])
            e_c = P.ev("dve")
            P.op("act", lambda a: a.activation(out=sq_dummy[:], in_=ln_c[:], func=AF.Square),
                 waits=[e_c, lw("ln_act"), lw("fsum_used")])
            e_sq = P.ev("act")
            P.op("dve", lambda v: v.tensor_reduce(
                out=ln_v[:], in_=sq_dummy[:], axis=AX.X, op=ALU.add),
                waits=[e_sq, lw("lnv_used")])
            e_vr = P.ev("dve")
            P.op("act", lambda a: a.activation(out=ln_v[:], in_=ln_v[:], func=AF.Ln,
                                               scale=1.0 / DM, bias=eps_sb[:]), waits=[e_vr])
            P.op("act", lambda a: a.activation(out=ln_v[:], in_=ln_v[:], func=AF.Exp, scale=-0.5))
            e_r = P.ev("act")
            last["ln_act"] = e_r
            last["fsum_used"] = e_r
            P.op("dve", lambda v: v.scalar_tensor_tensor(
                out=ln_c[:], in0=ln_c[:], scalar=ln_v[:], in1=ln_c[:],
                op0=ALU.mult, op1=ALU.bypass),
                 waits=[e_r])
            last["lnv_used"] = P.ev("dve")
            P.op("dve", lambda v, tc=tc: v.tensor_copy(
                out=acc[:, tc * DM:(tc + 1) * DM], in_=ln_c[:]))
            e_accw = P.ev("dve")
            if stop_after == "ffnln" and quarter == 0 and qi == 0:
                stg = sb("ffdump", [128, DM], F32, "dump")
                e_p = None
                row = 0
                for buf, nc_ in ((ln_c, DM), (ln_m, 1), (ln_v, 1), (fsum, DM), (fy1, DM)):
                    P.op("dve", lambda v, buf=buf, nc_=nc_: v.tensor_copy(
                        out=stg[:, 0:nc_], in_=buf[:, 0:nc_]),
                        waits=[e_accw] + ([e_p] if e_p else []))
                    e_c2 = P.ev("dve")
                    e_p = P.dma("sp", lambda s, row=row, nc_=nc_: s.dma_start(
                        out=out[row * 128:(row + 1) * 128, 0:nc_], in_=stg[:, 0:nc_]),
                        waits=[e_c2])
                    row += 1
                P.op("pool", lambda g: g.nop(), waits=[e_p])
                return nc, P, None
            e_q = quant_rows(ln_c[:], DM, [e_accw, lw("fq_used")], fq_bf[:, 0:DM], QF)
            last["lnc_used"] = e_q
            if stop_after == "ffnq" and quarter == 0 and qi == 0:
                stg = sb("ffdump", [128, DM], F32, "dump")
                e_p = None
                row = 0
                P.op("dve", lambda v: v.tensor_reduce(
                    out=QF["two"][:, 0:32], in_=ln_c[:].rearrange("p (g k) -> p g k", k=32),
                    axis=AX.X, op=ALU.max, apply_absolute_value=True), waits=[e_q])
                P.ev("dve")
                for buf, nc_ in ((QF["amax"], 32), (QF["e"], 32), (QF["scale"], 32),
                                 (QF["inv"], 32), (QF["r"], DM), (fq_bf, DM), (ln_c, DM),
                                 (QF["two"], 32)):
                    P.op("dve", lambda v, buf=buf, nc_=nc_: v.tensor_copy(
                        out=stg[:, 0:nc_], in_=buf[:, 0:nc_]),
                        waits=[e_q] + ([e_p] if e_p else []))
                    e_c2 = P.ev("dve")
                    e_p = P.dma("sp", lambda s, row=row, nc_=nc_: s.dma_start(
                        out=out[row * 128:(row + 1) * 128, 0:nc_], in_=stg[:, 0:nc_]),
                        waits=[e_c2])
                    row += 1
                P.op("pool", lambda g: g.nop(), waits=[e_p])
                return nc, P, None
            for g in range(KD // 4):
                j, pt, wwaits = rot(ptr, ptr_war, ptr_i)
                for q in range(4):
                    k = g * 4 + q
                    P.op("pe", lambda t, k=k, q=q, pt=pt: t.transpose(
                        out=pt[:, q * 128:(q + 1) * 128], in_=fq_bf[:, k * 128:(k + 1) * 128],
                        identity=ident[:]),
                        waits=([e_q] + wwaits) if q == 0 else [])
                e_tr = P.ev("pe")
                P.op("dve", lambda v, qi=qi, g=g, pt=pt: v.tensor_copy(
                    out=xqfm_sb[:, (qi * KD + g * 4) * 128:(qi * KD + g * 4 + 4) * 128],
                    in_=pt[:]),
                    waits=[e_tr, lw("xqfm_used") if (qi == 0 and g == 0) else None])
                ptr_war[j] = P.ev("dve")
            last["fq_used"] = P.ev("pe")
        e_xq = P.ev("dve")
        if stop_after == "ffn1" and quarter == 0:
            stg = sb("ffdump", [128, DM], F32, "dump")
            e_p = None
            row = 0
            for tc in range(4):
                P.op("dve", lambda v, tc=tc: v.tensor_copy(
                    out=stg[:], in_=acc[:, tc * DM:(tc + 1) * DM]),
                    waits=[e_xq] + ([e_p] if e_p else []))
                e_c2 = P.ev("dve")
                e_p = P.dma("sp", lambda s, row=row: s.dma_start(
                    out=out[row * 128:(row + 1) * 128, :], in_=stg[:]), waits=[e_c2])
                row += 1
            for (buf, nc_) in ((fq_bf, DM), (xqfm_sb, 4 * KD * 128)):
                for j in range(nc_ // DM):
                    P.op("dve", lambda v, buf=buf, j=j: v.tensor_copy(
                        out=stg[:], in_=buf[:, j * DM:(j + 1) * DM]), waits=[e_p])
                    e_c2 = P.ev("dve")
                    e_p = P.dma("sp", lambda s, row=row: s.dma_start(
                        out=out[row * 128:(row + 1) * 128, :], in_=stg[:]), waits=[e_c2])
                    row += 1
            P.op("pool", lambda g: g.nop(), waits=[e_p])
            return nc, P, None

        for f in range(8):
            wb = STR[:, (f % 2) * 4096:(f % 2) * 4096 + 4096]
            e_w = P.dma("sp", lambda s, f=f, wb=wb: s.dma_start(
                out=wb.rearrange("p (k q) -> p k q", k=KD),
                in_=wfcT_d[:, :, f * 512:(f + 1) * 512]),
                waits=[e_prep, lw(f"str{f % 2}")])
            psl = [rot(pz, pz_war, pz_i) for _ in range(4)]
            for k in range(KD):
                for qi in range(4):
                    i, pb, wwaits = psl[qi]
                    P.op("pe", lambda t, k=k, qi=qi, pb=pb, wb=wb: t.matmul(
                        out=pb[:], lhsT=xqfm_sb[:, (qi * KD + k) * 128:(qi * KD + k + 1) * 128],
                        rhs=wb[:, k * 512:(k + 1) * 512], start=(k == 0), stop=False),
                        waits=([e_w, e_xq] + wwaits) if k == 0 else [])
            for qi in range(4):
                i, pb, _ = psl[qi]
                P.op("pe", lambda t, qi=qi, pb=pb, f=f: t.matmul(
                    out=pb[:], lhsT=ones1[:], rhs=bfc_sb[0:1, f * 512:(f + 1) * 512],
                    start=False, stop=True),
                    waits=[e_ones, e_bias] if (f == 0 and qi == 0 and quarter == 0) else [])
            e_fmm = P.ev("pe")
            last[f"str{f % 2}"] = e_fmm
            for qi in range(4):
                i, pb, _ = psl[qi]
                P.op("act", lambda a, qi=qi, pb=pb: a.activation(
                    out=hge_sb[:, qi * 512:(qi + 1) * 512], in_=pb[:], func=AF.Gelu),
                    waits=[e_fmm, lw("hge_used")])
                pz_war[i] = P.ev("act")
            e_ge = P.ev("act")
            e_hq = quant_rows(hge_sb[:, 0:2048], 2048, [e_ge, lw("fq_used")],
                              fq_bf[:, 0:2048], QF)
            last["hge_used"] = e_hq
            for qi in range(4):
                j, pt, wwaits = rot(ptr, ptr_war, ptr_i)
                for jj in range(4):
                    P.op("pe", lambda t, qi=qi, jj=jj, pt=pt: t.transpose(
                        out=pt[:, jj * 128:(jj + 1) * 128],
                        in_=fq_bf[:, qi * 512 + jj * 128:qi * 512 + (jj + 1) * 128],
                        identity=ident[:]),
                        waits=([e_hq] + wwaits) if jj == 0 else [])
                e_tr = P.ev("pe")
                hq4 = hqfm_sb[:].rearrange("p (kk q c) -> p kk q c", q=4, c=128)
                P.op("dve", lambda v, f=f, qi=qi, pt=pt, hq4=hq4: v.tensor_copy(
                    out=hq4[:, f * 4:f * 4 + 4, qi],
                    in_=pt[:].rearrange("p (j c) -> p j c", c=128)),
                    waits=[e_tr, lw("hqfm_used") if (f == 0 and qi == 0) else None])
                ptr_war[j] = P.ev("dve")
            last["fq_used"] = P.ev("pe")
        e_hqf = P.ev("dve")
        if stop_after == "ffn2" and quarter == 0:
            stg = sb("ffdump", [128, DM], F32, "dump")
            e_p = None
            row = 0
            for j in range(2):
                P.op("dve", lambda v, j=j: v.tensor_copy(
                    out=stg[:], in_=hge_sb[:, j * DM:(j + 1) * DM]),
                    waits=[e_hqf] + ([e_p] if e_p else []))
                e_c2 = P.ev("dve")
                e_p = P.dma("sp", lambda s, row=row: s.dma_start(
                    out=out[row * 128:(row + 1) * 128, :], in_=stg[:]), waits=[e_c2])
                row += 1
            for j in range(14):
                P.op("dve", lambda v, j=j: v.tensor_copy(
                    out=stg[:], in_=hqfm_sb[:, j * DM:(j + 1) * DM]), waits=[e_p])
                e_c2 = P.ev("dve")
                e_p = P.dma("sp", lambda s, row=row: s.dma_start(
                    out=out[row * 128:(row + 1) * 128, :], in_=stg[:]), waits=[e_c2])
                row += 1
            P.op("pool", lambda g: g.nop(), waits=[e_p])
            return nc, P, None
        last["xqfm_used"] = e_hqf

        for fo in range(2):
            psl = [rot(pz, pz_war, pz_i) for _ in range(4)]
            for khalf in range(2):
                e_w = P.dma("sp", lambda s, fo=fo, khalf=khalf: s.dma_start(
                    out=STR[:].rearrange("p (k q) -> p k q", k=DCH),
                    in_=wprojT_d[:, khalf * DCH:(khalf + 1) * DCH, fo * 512:(fo + 1) * 512]),
                    waits=[e_prep, lw("str0"), lw("str1")])
                for k16 in range(DCH):
                    k = khalf * DCH + k16
                    for qi in range(4):
                        i, pb, wwaits = psl[qi]
                        P.op("pe", lambda t, k=k, k16=k16, qi=qi, pb=pb: t.matmul(
                            out=pb[:], lhsT=hqfm_sb[:, (k * 4 + qi) * 128:(k * 4 + qi + 1) * 128],
                            rhs=STR[:, k16 * 512:(k16 + 1) * 512], start=(k == 0), stop=False),
                            waits=([e_w, e_hqf] + wwaits) if k == 0 else ([e_w] if k16 == 0 else []))
                e_half = P.ev("pe")
                last["str0"] = e_half
                last["str1"] = e_half
            for qi in range(4):
                i, pb, _ = psl[qi]
                P.op("pe", lambda t, qi=qi, pb=pb, fo=fo: t.matmul(
                    out=pb[:], lhsT=ones1[:], rhs=bproj_sb[0:1, fo * 512:(fo + 1) * 512],
                    start=False, stop=True))
            e_pmm = P.ev("pe")
            last["hqfm_used"] = e_pmm
            for qi in range(4):
                i, pb, _ = psl[qi]
                P.op("act", lambda a, qi=qi, fo=fo, pb=pb: a.activation(
                    out=ffn_sb[:, qi * DM + fo * 512:qi * DM + (fo + 1) * 512], in_=pb[:],
                    func=AF.Copy), waits=[e_pmm, lw("ffn_used")])
                pz_war[i] = P.ev("act")
        e_ffno = P.ev("act")

        e_o = None
        for qi, tc in enumerate(tcs):
            P.op("dve", lambda v, qi=qi, tc=tc: v.tensor_tensor(
                out=ln_c[:], in0=ffn_sb[:, qi * DM:(qi + 1) * DM],
                in1=acc[:, tc * DM:(tc + 1) * DM], op=ALU.add),
                waits=[e_ffno, lw("lnc_used")])
            P.op("dve", lambda v: v.tensor_reduce(out=ln_m[:], in_=ln_c[:], axis=AX.X,
                                                  op=ALU.add))
            e_red = P.ev("dve")
            P.op("dve", lambda v: v.tensor_scalar_mul(out=ln_m[:], in0=ln_m[:], scalar1=1.0 / DM),
                 waits=[e_red])
            e_mean = P.ev("dve")
            P.op("dve", lambda v: v.scalar_tensor_tensor(
                out=ln_c[:], in0=ln_c[:], scalar=ln_m[:], in1=ln_c[:],
                op0=ALU.subtract, op1=ALU.bypass),
                 waits=[e_mean])
            e_c = P.ev("dve")
            P.op("act", lambda a: a.activation(out=sq_dummy[:], in_=ln_c[:], func=AF.Square),
                 waits=[e_c, lw("ln_act"), lw("fsum_used")])
            e_sq = P.ev("act")
            P.op("dve", lambda v: v.tensor_reduce(
                out=ln_v[:], in_=sq_dummy[:], axis=AX.X, op=ALU.add),
                waits=[e_sq, lw("lnv_used")])
            e_vr = P.ev("dve")
            P.op("act", lambda a: a.activation(out=ln_v[:], in_=ln_v[:], func=AF.Ln,
                                               scale=1.0 / DM, bias=eps_sb[:]), waits=[e_vr])
            P.op("act", lambda a: a.activation(out=ln_v[:], in_=ln_v[:], func=AF.Exp, scale=-0.5))
            e_r = P.ev("act")
            last["ln_act"] = e_r
            last["fsum_used"] = e_r
            P.op("dve", lambda v: v.scalar_tensor_tensor(
                out=ostage[:], in0=ln_c[:], scalar=ln_v[:], in1=ln_c[:],
                op0=ALU.mult, op1=ALU.bypass),
                 waits=[e_r, lw("ostage_used")])
            e_o = P.ev("dve")
            last["lnv_used"] = e_o
            last["lnc_used"] = e_o
            e_od = P.dma("sp", lambda s, tc=tc: s.dma_start(
                out=out[tc * 128:(tc + 1) * 128, :], in_=ostage[:]), waits=[e_o])
            last["ostage_used"] = e_od
        last["ffn_used"] = e_o

    P.op("pool", lambda g: g.nop(), waits=[last["ostage_used"]])
    return nc, P, None


def dump_sum(nc, P, sb, x_in, y_d, dirs, out):
    s0 = sb("dump0", [128, DM], F32, "dump")
    s1 = sb("dump1", [128, DM], F32, "dump")
    e_p = None
    e_c = None
    for tc in range(NTC):
        e_lx = P.dma("sp", lambda s, tc=tc: s.dma_start(
            out=s0[:], in_=x_in[tc * 128:(tc + 1) * 128, :]),
            waits=[e_p] if e_p else [])
        waits = [e_lx]
        for d in dirs:
            e_ly = P.dma("sp", lambda s, tc=tc, d=d: s.dma_start(
                out=s1[:], in_=y_d[d][:, tc, :]), waits=[e_c] if e_c else [])
            P.op("dve", lambda v: v.tensor_tensor(out=s0[:], in0=s0[:], in1=s1[:],
                                                  op=ALU.add), waits=waits + [e_ly])
            waits = []
            e_c = P.ev("dve")
        e_p = P.dma("sp", lambda s, tc=tc: s.dma_start(
            out=out[tc * 128:(tc + 1) * 128, :], in_=s0[:]), waits=[e_c])
    P.op("pool", lambda g: g.nop(), waits=[e_p])
    return None


SBUF_LIMIT = 229376 - 16384


def audit_sbuf(nc):
    worst = 0
    for fn in nc.m.functions:
        for a in fn.allocations:
            try:
                locs = a.memorylocations
            except Exception:
                continue
            for loc in locs:
                if "SB" in str(getattr(loc, "type", "")):
                    end = loc.addr + int(np.prod(list(loc.dims)[1:]))
                    if end > worst:
                        worst = end
                    assert end <= SBUF_LIMIT, f"SBUF overflow: {loc.name} ends at {end}"
    return worst


def emit(nc, P):
    audit_sbuf(nc)
    sems = {}
    dsems = {}
    cms = []
    for e in ENGS:
        cm = nc.semaphore(f"sem_{e}")
        sems[e] = cm.__enter__()
        cms.append(cm)
        if P.dma_cnt[e]:
            cm = nc.semaphore(f"dsem_{e}")
            dsems[e] = cm.__enter__()
            cms.append(cm)

    block_cm = nc.Block()
    block = block_cm.__enter__()

    def emit_queue(eng_obj, name):
        for rec in P.q[name]:
            for kind, weng, val in rec["waits"]:
                eng_obj.wait_ge(dsems[weng] if kind == "d" else sems[weng], val)
            inst = rec["fn"](eng_obj)
            if rec["dma"]:
                inst.then_inc(dsems[name], 16)
            elif rec["inc"] is not None:
                eng_obj.drain(fusable=False).then_inc(sems[name], 1)

    @block.sync
    def _(sync):
        emit_queue(sync, "sp")

    @block.tensor
    def _(tensor):
        emit_queue(tensor, "pe")

    @block.scalar
    def _(scalar):
        emit_queue(scalar, "act")

    @block.vector
    def _(vector):
        emit_queue(vector, "dve")

    @block.gpsimd
    def _(gpsimd):
        emit_queue(gpsimd, "pool")

    block_cm.__exit__(None, None, None)
    return sems, dsems


_IN_NAMES = [
    ("x", "L0_attn_x"),
    ("finw", "L0_attn_f_in_w"), ("fconvw", "L0_attn_f_conv_w"), ("fconvb", "L0_attn_f_conv_b"),
    ("fxproj", "L0_attn_f_xproj_w"), ("fdtw", "L0_attn_f_dt_w"), ("fdtb", "L0_attn_f_dt_b"),
    ("falog", "L0_attn_f_A_log"), ("fdpar", "L0_attn_f_D"), ("foutw", "L0_attn_f_out_w"),
    ("binw", "L0_attn_b_in_w"), ("bconvw", "L0_attn_b_conv_w"), ("bconvb", "L0_attn_b_conv_b"),
    ("bxproj", "L0_attn_b_xproj_w"), ("bdtw", "L0_attn_b_dt_w"), ("bdtb", "L0_attn_b_dt_b"),
    ("balog", "L0_attn_b_A_log"), ("bdpar", "L0_attn_b_D"), ("boutw", "L0_attn_b_out_w"),
    ("wfc", "L0_ffn_w_fc"), ("bfc", "L0_ffn_b_fc"), ("wproj", "L0_ffn_w_proj"),
    ("bproj", "L0_ffn_b_proj"),
]


def kernel(**inputs):
    nc, P, _ = build_program("full")
    emit(nc, P)
    xs = np.ascontiguousarray(inputs["L0_attn_x"], dtype=np.float32)
    shared = {short: np.ascontiguousarray(inputs[lname], dtype=np.float32)
              for short, lname in _IN_NAMES if short != "x"}
    in_maps = [dict(shared, x=xs[core]) for core in range(8)]
    res = run_bass_kernel_spmd(nc, in_maps, core_ids=list(range(8)))
    return np.stack([r["out"] for r in res.results], axis=0)
